# version 02 (final)
# baseline (speedup 1.0000x reference)
import sys
sys.path.insert(0, "/opt/trn_rl_repo")

"""Library for the mk_macaron_d2 megakernel on trn2. Feature-major layout.

SBUF budget (bytes/partition, ~213KB usable):
  xT      8 x [128,2048] f32   64KB   residual stream
  B1/B2/B3 24 x [128,2048] bf16 96KB  working arenas
  SCR     [128,2048] f32        8KB   squares / token-tile views / iota
  REP,REPQ [128,2048] bf16      8KB   LN replicas
  WBUF    [128,8192] bf16      16KB   weight col-block double buffer
  rows    [65,2048] f32         8KB   stat rows at partition 0/32/64
  consts+misc                  ~6KB
PSUM (8 banks): big [128,2048]f32 =4, mm 2x[128,512]f32 =2, ptf [128,512]f32 =1,
  ptb [128,256]bf16 =1.
"""
import contextlib
import numpy as np
import concourse.bass as bass
import concourse.mybir as mybir
from concourse.bass_core_rust import IndirectOffsetOnAxis

F32 = mybir.dt.float32
BF16 = mybir.dt.bfloat16
I32 = mybir.dt.int32
U8 = mybir.dt.uint8
AF = mybir.ActivationFunctionType
ALU = mybir.AluOpType
AX = mybir.AxisListType

T = 2048
D = 1024
NCH = D // 128
NTT = T // 128
NS = T // 512
ENGS = ("t", "v", "s", "g", "y")
ENG_NAMES = {"t": "tensor", "v": "vector", "s": "scalar", "g": "gpsimd", "y": "sync"}


class Sem:
    def __init__(self, handle):
        self.h = handle
        self.n = 0


class Prog:
    def __init__(self, nc):
        self.nc = nc
        self.streams = {e: [] for e in ENGS}
        self.stack = contextlib.ExitStack()
        self._sems = {}
        self.dirty = set()
        self.bar = self.sem("bar")
        self.dma_hw = self.sem("dmahw")
        self.dma_sw = self.sem("dmasw")
        self.cc_s = self.sem("cc")
        self.sb = None

    def sem(self, name):
        if name not in self._sems:
            h = self.stack.enter_context(self.nc.semaphore(name))
            self._sems[name] = Sem(h)
        return self._sems[name]

    def emit(self, eng, fn, inc=None, k=1):
        self.dirty.add(eng)
        if inc is not None:
            inc.n += k
            def wrapped(e, fn=fn, inc=inc, k=k):
                fn(e).then_inc(inc.h, k)
            self.streams[eng].append(wrapped)
            return inc.n
        self.streams[eng].append(fn)
        return None

    def dma(self, eng, out, in_, sem=None, **kw):
        s = sem if sem is not None else (self.dma_sw if eng == "g" else self.dma_hw)
        return self.emit(eng, lambda e, out=out, in_=in_, kw=kw: e.dma_start(out=out, in_=in_, **kw),
                         inc=s, k=16)

    def idma(self, sem=None, **kw):
        s = sem if sem is not None else self.dma_sw
        return self.emit("g", lambda e, kw=kw: e.indirect_dma_start(**kw), inc=s, k=16)

    def collective(self, ins, outs, kind="AllGather", op=ALU.bypass):
        return self.emit("g", lambda e, ins=ins, outs=outs: e.collective_compute(
            kind, op, replica_groups=[list(range(8))], ins=ins, outs=outs),
            inc=self.cc_s, k=1)

    def wait(self, eng, sem, val=None):
        v = sem.n if val is None else val
        if v > 0:
            self.emit(eng, lambda e, sem=sem, v=v: e.wait_ge(sem.h, v))

    def _nop_inc(self, e, eng, sem, k):
        if eng == "v":
            e.tensor_copy(self.sb.nops[:, 2:4], self.sb.nops[:, 0:2]).then_inc(sem, k)
        elif eng == "g":
            e.tensor_copy(self.sb.nops[:, 6:8], self.sb.nops[:, 4:6]).then_inc(sem, k)
        elif eng == "s":
            e.activation(self.sb.nops[:, 8:10], self.sb.nops[:, 0:2], AF.Copy).then_inc(sem, k)
        elif eng == "t":
            e.matmul(self.sb.noppsum[0:2, 0:2], self.sb.nops[0:1, 0:2],
                     self.sb.nops[0:1, 0:2], start=True, stop=True).then_inc(sem, k)

    def barrier(self):
        producers = sorted(self.dirty - {"y"})
        self.dirty = set()
        self.bar.n += len(producers)
        bar_tgt = self.bar.n
        if not hasattr(self, "_bar_seen"):
            self._bar_seen = {}
        waits = []
        for name, sem in self._sems.items():
            if sem is self.bar:
                continue
            if sem.n > self._bar_seen.get(name, 0):
                self._bar_seen[name] = sem.n
                waits.append((sem.h, sem.n))
        waits = tuple(waits)
        for eng in ENGS:
            is_prod = eng in producers
            def fn(e, is_prod=is_prod, eng=eng, bar_tgt=bar_tgt, waits=waits):
                if is_prod:
                    self._nop_inc(e, eng, self.bar.h, 1)
                e.wait_ge(self.bar.h, bar_tgt)
                for (h, v) in waits:
                    e.wait_ge(h, v)
            self.streams[eng].append(fn)

    def finalize(self):
        block = self.stack.enter_context(self.nc.Block())
        for eng in ENGS:
            fns = self.streams[eng]
            def runner(e, fns=fns):
                for f in fns:
                    f(e)
            getattr(block, ENG_NAMES[eng])(runner)
        self.stack.close()


def sbuf(prog, name, shape, dtype):
    return prog.stack.enter_context(prog.nc.sbuf_tensor(name, shape, dtype))


def psum(prog, name, shape, dtype=F32):
    return prog.stack.enter_context(prog.nc.psum_tensor(name, shape, dtype))


class SB:
    pass


def alloc_common(prog):
    sb = SB()
    prog.sb = sb
    sb.nops = sbuf(prog, "nopbuf", [1, 12], F32)
    sb.noppsum = psum(prog, "noppsum", [2, 2], F32)
    sb.xT = [sbuf(prog, f"xT{k}", [128, T], F32) for k in range(NCH)]
    sb.B1 = [sbuf(prog, f"B1_{k}", [128, T], BF16) for k in range(NCH)]
    sb.B2 = [sbuf(prog, f"B2_{k}", [128, T], BF16) for k in range(NCH)]
    sb.B3 = [sbuf(prog, f"B3_{k}", [128, T], BF16) for k in range(NCH)]
    sb.SCR = sbuf(prog, "SCR", [128, T], F32)
    sb.sq = sb.SCR
    sb.tok = [sb.SCR[:, 0:1024], sb.SCR[:, 1024:2048]]
    sb.rep = sbuf(prog, "rep", [128, T], BF16)
    sb.repQ = sbuf(prog, "repQ", [128, T], BF16)
    sb.wbuf = sbuf(prog, "wbuf", [128, 4608], BF16)
    sb.wbufs = [sb.wbuf[:, 0:2048], sb.wbuf[:, 2048:4096]]
    sb.row_m = sbuf(prog, "row_m", [1, T], F32)[:, :]
    sb.row_a = sbuf(prog, "row_a", [1, T], F32)[:, :]
    sb.ones128 = sbuf(prog, "ones128", [128, 2], F32)
    sb.ones1 = sbuf(prog, "ones1", [1, 128], F32)
    sb.ident_bf = sbuf(prog, "ident_bf", [128, 128], BF16)
    sb.ident_f = sbuf(prog, "ident_f", [128, 128], F32)
    sb.iota_i = sbuf(prog, "iota_i", [128, 128], I32)
    sb.iota_col = sbuf(prog, "iota_col", [128, 1], I32)
    sb.iota_colf = sbuf(prog, "iota_colf", [128, 1], F32)
    sb.pp = sbuf(prog, "pp", [128, 80], F32)
    sb.onesb = sbuf(prog, "onesb", [128, 130], BF16)
    sb.strictup = sbuf(prog, "strictup", [128, 128], BF16)
    sb.iota_colb = sbuf(prog, "iota_colb", [128, 1], BF16)
    sb.lrow = sb.SCR[0:16, 0:512]
    sb.big = psum(prog, "big", [128, 1024], F32)
    sb.mm = [psum(prog, f"mm{i}", [128, 512], F32) for i in range(2)]
    sb.flex = psum(prog, "flex", [128, 1024], F32)
    sb.flexb = psum(prog, "flexb", [128, 1024], BF16)
    return sb


def emit_init_consts(prog, sb):
    def f(e):
        e.memset(sb.ones128[:, :], 1.0)
        e.memset(sb.ones1[:, :], 1.0)
        e.memset(sb.nops[:, :], 0.0)
    prog.emit("v", f)
    def g(e):
        e.memset(sb.nops[:, 4:8], 0.0)
        e.iota(sb.iota_i[:, :], pattern=[[1, 128]], base=0, channel_multiplier=0)
        e.iota(sb.iota_col[:, :], pattern=[[0, 1]], base=0, channel_multiplier=1)
    prog.emit("g", g)
    prog.barrier()
    def h2(e):
        e.tensor_copy(sb.iota_colf[:, :], sb.iota_col[:, :])
        e.tensor_copy(sb.SCR[:, 0:128], sb.iota_i[:, :])
    prog.emit("v", h2)
    prog.barrier()
    def h3(e):
        e.tensor_scalar(sb.ident_f[:, :], sb.SCR[:, 0:128], sb.iota_colf[:, :], 0.0,
                        op0=ALU.subtract, op1=ALU.is_equal)
        e.memset(sb.onesb[:, :], 1.0)
        e.tensor_scalar(sb.strictup[:, :], sb.SCR[:, 0:128], sb.iota_colf[:, :],
                        None, op0=ALU.is_gt)
        e.tensor_copy(sb.iota_colb[:, :], sb.iota_colf[:, :])
    prog.emit("v", h3)
    prog.barrier()
    prog.emit("v", lambda e: e.tensor_copy(sb.ident_bf[:, :], sb.ident_f[:, :]))
    prog.barrier()


def emit_load_xT(prog, sb, x_dram, dst=None):
    dst = dst or sb.xT
    sL = [prog.sem("p_load0"), prog.sem("p_load1")]
    sT = prog.sem("p_tr")
    sE = prog.sem("p_evac")
    tr_done = {}
    ev_done = {}
    base_T = sT.n
    base_E = sE.n
    for tt in range(NTT):
        if tt >= 2:
            prog.wait("y", sT, tr_done[tt - 2])
        ld = prog.dma("y", sb.tok[tt % 2], x_dram[tt * 128:(tt + 1) * 128, :], sem=sL[tt % 2])
        prog.wait("t", sL[tt % 2], ld)
        if tt >= 1:
            prog.wait("t", sE, ev_done[tt - 1])
        def trg(e, tt=tt):
            last = None
            for k in range(NCH):
                last = e.transpose(sb.flex[:, k * 128:(k + 1) * 128],
                                   sb.tok[tt % 2][:, k * 128:(k + 1) * 128],
                                   sb.ident_f[:, :])
            return last
        tr_done[tt] = prog.emit("t", trg, inc=sT)
        prog.wait("v", sT, tr_done[tt])
        def evg(e, tt=tt):
            last = None
            for k in range(NCH):
                last = e.tensor_copy(dst[k][:, tt * 128:(tt + 1) * 128],
                                     sb.flex[:, k * 128:(k + 1) * 128])
            return last
        ev_done[tt] = prog.emit("v", evg, inc=sE)
    prog.barrier()


def emit_store_fm(prog, sb, out_dram, src):
    sT = prog.sem("p_tr")
    sE = prog.sem("p_evac")
    sS = [prog.sem("p_store0"), prog.sem("p_store1")]
    tr_done = {}
    ev_done = {}
    st_done = {}
    for tt in range(NTT):
        if tt >= 2:
            prog.wait("v", sS[tt % 2], st_done[tt - 2])
        if tt >= 1:
            prog.wait("t", sE, ev_done[tt - 1])
        def trg(e, tt=tt):
            last = None
            for k in range(NCH):
                last = e.transpose(sb.flex[:, k * 128:(k + 1) * 128],
                                   src[k][:, tt * 128:(tt + 1) * 128],
                                   sb.ident_f[:, :])
            return last
        tr_done[tt] = prog.emit("t", trg, inc=sT)
        prog.wait("v", sT, tr_done[tt])
        ev_done[tt] = prog.emit("v", lambda e, tt=tt: e.tensor_copy(
            sb.tok[tt % 2], sb.flex[:, :]), inc=sE)
        prog.wait("y", sE, ev_done[tt])
        st_done[tt] = prog.dma("y", out_dram[tt * 128:(tt + 1) * 128, :],
                               sb.tok[tt % 2], sem=sS[tt % 2])
    prog.barrier()


def _sum_chain(prog, sb, tile_fn, out_row):
    onecol = sb.ones128[:, 0:1] if tile_fn(0).dtype == F32 else sb.onesb[:, 128:129]
    for h in range(2):
        off = h * 1024
        for k in range(NCH):
            def mmf(e, k=k, off=off, onecol=onecol):
                last = None
                for ns in range(2):
                    sl = slice(off + ns * 512, off + (ns + 1) * 512)
                    pl = slice(ns * 512, (ns + 1) * 512)
                    last = e.matmul(sb.big[0:1, pl], onecol,
                                    tile_fn(k)[:, sl], start=(k == 0), stop=(k == NCH - 1))
                return last
            prog.emit("t", mmf)
        prog.barrier()
        prog.emit("v", lambda e, off=off: e.tensor_copy(out_row[0:1, off:off + 1024],
                                                        sb.big[0:1, :]))
        prog.barrier()


def _sumsq_chain(prog, sb, src, out_row):
    sq_s = prog.sem("sq_s")
    sq_t = prog.sem("sq_t")
    for h in range(2):
        off = h * 1024
        for k in range(NCH):
            if not (h == 0 and k == 0):
                prog.wait("s", sq_t, sq_t.n)
            sqid = prog.emit("s", lambda e, k=k, off=off: e.activation(
                sb.sq[:, 0:1024], src[k][:, off:off + 1024], AF.Square), inc=sq_s)
            prog.wait("t", sq_s, sqid)
            def mmf(e, k=k):
                last = None
                for ns in range(2):
                    pl = slice(ns * 512, (ns + 1) * 512)
                    last = e.matmul(sb.big[0:1, pl], sb.ones128[:, 0:1],
                                    sb.sq[:, pl], start=(k == 0), stop=(k == NCH - 1))
                return last
            prog.emit("t", mmf, inc=sq_t)
        prog.barrier()
        prog.emit("v", lambda e, off=off: e.tensor_copy(out_row[0:1, off:off + 1024],
                                                        sb.big[0:1, :]))
        prog.barrier()


def _rstd_from_var(prog, sb, var_row, out_row):
    prog.emit("s", lambda e: e.activation(out_row, var_row, AF.Ln))
    prog.barrier()
    prog.emit("s", lambda e: e.activation(out_row, out_row, AF.Exp, scale=-0.5))
    prog.barrier()


def _replicate_row(prog, sb, row, dst_sbuf):
    for h in range(2):
        off = h * 1024
        def rep(e, off=off):
            last = None
            for ns in range(2):
                last = e.matmul(sb.big[:, ns * 512:(ns + 1) * 512], sb.ones1[0:1, :],
                                row[0:1, off + ns * 512:off + (ns + 1) * 512],
                                start=True, stop=True)
            return last
        prog.emit("t", rep)
        prog.barrier()
        prog.emit("v", lambda e, off=off: e.tensor_copy(dst_sbuf[:, off:off + 1024],
                                                        sb.big[:, :]))
        prog.barrier()


def emit_ln(prog, sb, src, out, eps=1e-5, gamma=None, beta=None):
    _sum_chain(prog, sb, lambda k: src[k], sb.row_m)
    _sumsq_chain(prog, sb, src, sb.row_a)
    def rows(e):
        e.tensor_scalar(sb.row_m, sb.row_m, 1.0 / D, None, op0=ALU.mult)
        e.tensor_scalar(sb.row_a, sb.row_a, 1.0 / D, None, op0=ALU.mult)
        e.scalar_tensor_tensor(sb.row_a, sb.row_m, 1.0, sb.row_m, op0=ALU.bypass,
                               op1=ALU.mult) if False else None
        e.tensor_mul(sb.SCR[0:1, 0:T], sb.row_m, sb.row_m)
        e.tensor_sub(sb.row_a, sb.row_a, sb.SCR[0:1, 0:T])
        e.tensor_scalar(sb.row_a, sb.row_a, float(eps), None, op0=ALU.add)
    prog.emit("v", rows)
    prog.barrier()
    _rstd_from_var(prog, sb, sb.row_a, sb.row_a)
    prog.emit("v", lambda e: e.tensor_mul(sb.row_m, sb.row_m, sb.row_a))
    prog.barrier()
    _replicate_row(prog, sb, sb.row_a, sb.rep)
    _replicate_row(prog, sb, sb.row_m, sb.repQ)
    for k in range(NCH):
        def ap(e, k=k):
            e.tensor_mul(sb.sq[:, :], src[k][:, :], sb.rep[:, :])
            if gamma is None:
                e.tensor_sub(out[k][:, :], sb.sq[:, :], sb.repQ[:, :])
            else:
                e.tensor_sub(sb.sq[:, :], sb.sq[:, :], sb.repQ[:, :])
                e.tensor_scalar(out[k][:, :], sb.sq[:, :], gamma[k], beta[k],
                                op0=ALU.mult, op1=ALU.add)
        prog.emit("v", ap)
        prog.barrier()


def emit_rms(prog, sb, src, out, weight_cols, eps=1e-5):
    _sumsq_chain(prog, sb, src, sb.row_a)
    def rows(e):
        e.tensor_scalar(sb.row_a, sb.row_a, 1.0 / D, float(eps), op0=ALU.mult, op1=ALU.add)
    prog.emit("v", rows)
    prog.barrier()
    _rstd_from_var(prog, sb, sb.row_a, sb.row_a)
    _replicate_row(prog, sb, sb.row_a, sb.rep)
    for k in range(NCH):
        def ap(e, k=k):
            e.tensor_mul(sb.sq[:, :], src[k][:, :], sb.rep[:, :])
            e.tensor_scalar(out[k][:, :], sb.sq[:, :], weight_cols[k], None, op0=ALU.mult)
        prog.emit("v", ap)
        prog.barrier()


def emit_load_perch(prog, sb, dst, vec_dram, ncols=NCH):
    prog.dma("g", dst, vec_dram.rearrange("(k p) -> p k", p=128),
             allow_slow_non_contiguous=True)


def emit_fm_to_tm(prog, sb, src, dst_slot, nch=NCH):
    sT = prog.sem("p_tr")
    sE = prog.sem("p_evac")
    tr_done = {}
    ev_done = {}
    for tt in range(NTT):
        if tt >= 1:
            prog.wait("t", sE, ev_done[tt - 1])
        def trg(e, tt=tt):
            last = None
            for k in range(nch):
                last = e.transpose(sb.flexb[:, k * 128:(k + 1) * 128],
                                   src[k][:, tt * 128:(tt + 1) * 128],
                                   sb.ident_bf[:, :])
            return last
        tr_done[tt] = prog.emit("t", trg, inc=sT)
        prog.wait("v", sT, tr_done[tt])
        ev_done[tt] = prog.emit("v", lambda e, tt=tt, nch=nch: e.tensor_copy(
            dst_slot(tt), sb.flexb[:, 0:nch * 128]), inc=sE)
    prog.barrier()


def emit_tm_to_fm(prog, sb, src_slot, dst, nch=NCH, add_scale=None):
    sT = prog.sem("p_tr")
    sE = prog.sem("p_evac")
    tr_done = {}
    ev_done = {}
    for tt in range(NTT):
        if tt >= 1:
            prog.wait("t", sE, ev_done[tt - 1])
        def trg(e, tt=tt):
            last = None
            for k in range(nch):
                last = e.transpose(sb.flexb[:, k * 128:(k + 1) * 128],
                                   src_slot(tt)[:, k * 128:(k + 1) * 128],
                                   sb.ident_bf[:, :])
            return last
        tr_done[tt] = prog.emit("t", trg, inc=sT)
        prog.wait("v", sT, tr_done[tt])
        def evg(e, tt=tt):
            last = None
            for k in range(nch):
                if add_scale is None:
                    last = e.tensor_copy(dst[k][:, tt * 128:(tt + 1) * 128],
                                         sb.flexb[:, k * 128:(k + 1) * 128])
                else:
                    last = e.scalar_tensor_tensor(
                        dst[k][:, tt * 128:(tt + 1) * 128],
                        sb.flexb[:, k * 128:(k + 1) * 128], float(add_scale),
                        dst[k][:, tt * 128:(tt + 1) * 128], op0=ALU.mult, op1=ALU.add)
            return last
        ev_done[tt] = prog.emit("v", evg, inc=sE)
    prog.barrier()


def emit_proj_fm(prog, sb, rhs, w_bf_dram, out_evac, K, F, evac_eng="v",
                 n_slices=NS, sl_off=0, bias_cols=None, act=None, out_tiles=None,
                 out_bias_res=None):
    KC = K // 128
    FC = F // 128
    sW = [prog.sem("prj_w0"), prog.sem("prj_w1")]
    sMM = prog.sem("prj_m")
    sEV = prog.sem("prj_e")
    base_mm = sMM.n
    base_ev = sEV.n
    w_view = w_bf_dram.rearrange("(kc p) f -> p kc f", p=128)
    if KC * 128 <= 2048:
        wb = [sb.wbufs[0][:, 0:KC * 128], sb.wbufs[1][:, 0:KC * 128]]
        depth = 2
    else:
        wb = [sb.wbuf[:, 0:KC * 128], sb.wbuf[:, 0:KC * 128]]
        depth = 1
    for fc in range(FC):
        if fc >= depth:
            prog.wait("y", sMM, base_mm + (fc - depth + 1) * n_slices)
        ldid = prog.dma("y", wb[fc % 2].rearrange("p (kc c) -> p kc c", c=128),
                        w_view[:, :, fc * 128:(fc + 1) * 128], sem=sW[fc % 2])
        prog.wait("t", sW[fc % 2], ldid)
        for ns in range(n_slices):
            gi = fc * n_slices + ns
            if gi >= 2:
                prog.wait("t", sEV, base_ev + gi - 1)
            ps = sb.mm[gi % 2]
            def mmf(e, fc=fc, ns=ns, ps=ps):
                sl = slice(sl_off + ns * 512, sl_off + (ns + 1) * 512)
                last = None
                for kc in range(KC):
                    last = e.matmul(ps[:, :], wb[fc % 2][:, kc * 128:(kc + 1) * 128],
                                    rhs[kc][:, sl], start=(kc == 0), stop=(kc == KC - 1))
                return last
            mmid = prog.emit("t", mmf, inc=sMM)
            prog.wait(evac_eng, sMM, mmid)
            if out_evac is not None:
                prog.emit(evac_eng, lambda e, fc=fc, ns=ns, ps=ps: out_evac(e, fc, ns, ps),
                          inc=sEV)
            else:
                def ev(e, fc=fc, ns=ns, ps=ps):
                    sl = slice(ns * 512, (ns + 1) * 512)
                    src = ps[:, :]
                    if out_bias_res is not None:
                        res_tiles, scale = out_bias_res
                        if bias_cols is not None:
                            e.tensor_scalar(sb.sq[:, 0:512], src, bias_cols[fc], None,
                                            op0=ALU.add)
                            src2 = sb.sq[:, 0:512]
                        else:
                            src2 = src
                        return e.scalar_tensor_tensor(
                            out_tiles[fc][:, sl], src2, float(scale),
                            res_tiles[fc][:, sl], op0=ALU.mult, op1=ALU.add)
                    if act is not None:
                        return e.activation(out_tiles[fc][:, sl], src, act,
                                            bias=(bias_cols[fc] if bias_cols else 0.0))
                    if bias_cols is not None:
                        return e.tensor_scalar(out_tiles[fc][:, sl], src, bias_cols[fc],
                                               None, op0=ALU.add)
                    return e.tensor_copy(out_tiles[fc][:, sl], src)
                prog.emit(evac_eng, ev, inc=sEV)
    prog.barrier()


def emit_router(prog, sb, xn, wrT, rbias_col, logits_dst_dram):
    sMM = prog.sem("prj_m")
    sEV = prog.sem("prj_e")
    sST = [prog.sem("rt_s0"), prog.sem("rt_s1")]
    base_ev = sEV.n
    st_ids = {}
    for ns in range(NS):
        if ns >= 2:
            prog.wait("t", sEV, base_ev + ns - 1)
        ps = sb.mm[ns % 2]
        def mmf(e, ns=ns, ps=ps):
            sl = slice(ns * 512, (ns + 1) * 512)
            last = None
            for kc in range(NCH):
                last = e.matmul(ps[0:16, :], wrT[:, kc * 16:(kc + 1) * 16],
                                xn[kc][:, sl], start=(kc == 0), stop=(kc == NCH - 1))
            return last
        mmid = prog.emit("t", mmf, inc=sMM)
        prog.wait("v", sMM, mmid)
        if ns >= 1:
            prog.wait("v", sST[(ns - 1) % 2], st_ids[ns - 1])
        evid = prog.emit("v", lambda e, ps=ps: e.tensor_scalar(
            sb.lrow[:, :], ps[0:16, :], rbias_col, None, op0=ALU.add), inc=sEV)
        prog.wait("y", sEV, evid)
        st_ids[ns] = prog.dma("y", logits_dst_dram[:, ns * 512:(ns + 1) * 512],
                              sb.lrow, sem=sST[ns % 2])
    prog.barrier()


"""Phase builders for the macaron megakernel."""
import numpy as np
import concourse.bass as bass
import concourse.mybir as mybir
from concourse.library_config import mlp as mlp_lib, standard as std_lib
from concourse.library_overlay import lower_extended_insts


def emit_pre_moe(prog, sb, x_src, nw_dram, wr_dram, rb_dram, xn_tok_dst, logits_dst):
    nc = prog.nc
    nw_col = sbuf(prog, "nw_col", [128, NCH], F32)
    wrT = sbuf(prog, "wrT", [128, NCH * 16], BF16)
    rb_col = sbuf(prog, "rb_col", [16, 1], F32)
    emit_load_perch(prog, sb, nw_col[:, :], nw_dram)
    for k in range(NCH):
        prog.dma("g", wrT[:, k * 16:(k + 1) * 16],
                 wr_dram[:, k * 128:(k + 1) * 128].rearrange("e p -> p e"),
                 allow_slow_non_contiguous=True)
    prog.dma("g", rb_col[:, :], rb_dram.rearrange("(e u) -> e u", u=1))
    prog.barrier()

    emit_ln(prog, sb, x_src, sb.B1)
    emit_rms(prog, sb, sb.B1, sb.B2, [nw_col[:, k:k + 1] for k in range(NCH)])
    def slot(tt):
        return sb.B3[tt // 2][:, (tt % 2) * 1024:(tt % 2 + 1) * 1024]
    emit_fm_to_tm(prog, sb, sb.B2, slot)
    for tt in range(NTT):
        prog.dma("y", xn_tok_dst[tt * 128:(tt + 1) * 128, :], slot(tt))
    emit_router(prog, sb, sb.B2, wrT, rb_col[:, :], logits_dst)
    prog.barrier()


def build_launch_pre1():
    nc = bass.Bass("TRN2", target_bir_lowering=False, detect_race_conditions=False)
    x_d = nc.dram_tensor("x", [T, D], F32, kind="ExternalInput").ap()
    nw_d = nc.dram_tensor("nw", [D], F32, kind="ExternalInput").ap()
    wr_d = nc.dram_tensor("wr", [16, D], F32, kind="ExternalInput").ap()
    rb_d = nc.dram_tensor("rb", [16], F32, kind="ExternalInput").ap()
    xn_o = nc.dram_tensor("xn_tok", [T, D], BF16, kind="ExternalOutput").ap()
    lg_o = nc.dram_tensor("logits", [16, T], F32, kind="ExternalOutput").ap()
    prog = Prog(nc)
    sb = alloc_common(prog)
    emit_init_consts(prog, sb)
    emit_load_xT(prog, sb, x_d)
    emit_pre_moe(prog, sb, sb.xT, nw_d, wr_d, rb_d, xn_o, lg_o)
    prog.finalize()
    return nc


import os

def build_debug(stage):
    nc = bass.Bass("TRN2", target_bir_lowering=False, detect_race_conditions=False)
    x_d = nc.dram_tensor("x", [T, D], F32, kind="ExternalInput").ap()
    nw_d = nc.dram_tensor("nw", [D], F32, kind="ExternalInput").ap()
    wr_d = nc.dram_tensor("wr", [16, D], F32, kind="ExternalInput").ap()
    rb_d = nc.dram_tensor("rb", [16], F32, kind="ExternalInput").ap()
    xo = nc.dram_tensor("xo", [T, D], F32, kind="ExternalOutput").ap()
    prog = Prog(nc)
    sb = alloc_common(prog)
    emit_init_consts(prog, sb)
    if stage == 0:
        prog.dma("g", xo[0:128, 0:128], sb.ident_f[:, :])
        prog.barrier()
        prog.finalize()
        return nc
    if stage >= 2:
        emit_load_xT(prog, sb, x_d)
    if stage >= 3:
        nw_col = sbuf(prog, "nw_col", [128, NCH], F32)
        wrT = sbuf(prog, "wrT", [128, NCH * 16], BF16)
        rb_col = sbuf(prog, "rb_col", [16, 1], F32)
        if stage >= 4:
            emit_load_perch(prog, sb, nw_col[:, :], nw_dram=nw_d)
            for k in range(NCH):
                prog.dma("g", wrT[:, k * 16:(k + 1) * 16],
                         wr_d[:, k * 128:(k + 1) * 128].rearrange("e p -> p e"),
                         allow_slow_non_contiguous=True)
            prog.dma("g", rb_col[:, :], rb_d.rearrange("(e u) -> e u", u=1))
            prog.barrier()
        if stage >= 5:
            emit_ln(prog, sb, sb.xT, sb.B1)
    emit_store_fm(prog, sb, xo, sb.xT)
    prog.finalize()
    return nc


BIG_SLOT = 60000.0


def emit_moe_weight_prep(prog, sb, fc1_f32, fc2_f32, fc1T_bf, fc2T_bf, fc_bf_tmp):
    sP = [prog.sem("wp0"), prog.sem("wp1")]
    cast_ids = []
    for j in range(2):
        cast_ids.append(prog.dma("g", fc_bf_tmp[0][j], fc1_f32[j], sem=sP[0]))
        cast_ids.append(prog.dma("g", fc_bf_tmp[1][j], fc2_f32[j], sem=sP[1]))
    fc1_f32 = None
    prog.barrier()
    sL = [prog.sem("wp2"), prog.sem("wp3")]
    sS = [prog.sem("wp4"), prog.sem("wp5")]
    i = 0
    for j in range(2):
        for (src, dstT, R, CC) in ((fc_bf_tmp[0][j], fc1T_bf[j], 4096, 8),
                                   (fc_bf_tmp[1][j], fc2T_bf[j], 1024, 32)):
            for ch in range(CC):
                ld = prog.dma("y", sb.wbuf[:, 0:R],
                              src[:, ch * 128:(ch + 1) * 128], sem=sL[0],
                              transpose=True)
                prog.wait("y", sL[0], ld)
                st = prog.dma("y", dstT[ch * 128:(ch + 1) * 128, :],
                              sb.wbuf[:, 0:R], sem=sS[0])
                prog.wait("y", sS[0], st)
                i += 1
    prog.barrier()


def emit_moe_experts(prog, sb, lg_my, esel_dram, fc1b, fc2b):
    nc = prog.nc
    NTOK = 16384
    moe_esel = sbuf(prog, "moe_esel", [128, 32], F32)
    moe_lg = sbuf(prog, "moe_lg", [128, 256], F32)
    moe_st = sbuf(prog, "moe_st", [128, 16], F32)
    moe_u8 = sbuf(prog, "moe_u8", [128, 256], U8)
    moe_mb = sbuf(prog, "moe_mb", [128, 256], BF16)
    moe_idx = sbuf(prog, "moe_idx", [128, 16], I32)
    moe_sc = sbuf(prog, "moe_sc", [128, 16], F32)
    moe_g = sbuf(prog, "moe_g", [128, 16], F32)
    b1c = sbuf(prog, "b1c", [128, 64], F32)
    b2c = sbuf(prog, "b2c", [128, 16], F32)
    LO, HI, MID, CNT = 0, 2, 4, 6
    for j in range(2):
        prog.dma("g", b1c[:, j * 32:(j + 1) * 32],
                 fc1b[j].rearrange("(k p) -> p k", p=128), allow_slow_non_contiguous=True)
        prog.dma("g", b2c[:, j * 8:(j + 1) * 8],
                 fc2b[j].rearrange("(k p) -> p k", p=128), allow_slow_non_contiguous=True)
        prog.dma("g", moe_esel[:, j * 16:(j + 1) * 16], esel_dram[j])
    for j in range(2):
        prog.dma("y", moe_lg[:, j * 128:(j + 1) * 128],
                 lg_my[j].rearrange("(p f) -> p f", p=128))
    prog.barrier()

    def binit(e):
        e.memset(moe_st[:, 0:4], -1e4)
        e.memset(moe_st[:, 4:8], 1e4)
        e.memset(moe_sc[:, :], 0.0)
        return e.memset(moe_m32[:, :], 0)
    prog.emit("v", binit)
    prog.barrier()
    sB = prog.sem("moe_bis")
    sBt = prog.sem("moe_bist")
    for it in range(40):
        vid = prog.emit("v", lambda e: e.tensor_add(moe_st[:, 8:12], moe_st[:, 0:4],
                                                    moe_st[:, 4:8]), inc=sB)
        prog.wait("v", sB, vid)
        prog.barrier()
        vid2 = prog.emit("v", lambda e: e.tensor_scalar(moe_st[:, 8:12], moe_st[:, 8:12],
                                                        0.5, None, op0=ALU.mult), inc=sB)
        prog.wait("v", sB, vid2)
        prog.barrier()
        def cmask2(e):
            for j in range(2):
                e.tensor_scalar(moe_mb[:, j * 128:(j + 1) * 128],
                                moe_lg[:, j * 128:(j + 1) * 128],
                                moe_st[:, 8 + 2 * j:9 + 2 * j], None, op0=ALU.is_ge)
            return e.tensor_reduce(moe_sc[:, 0:2],
                                   moe_mb[:].rearrange("p (j f) -> p j f", j=2),
                                   axis=AX.X, op=ALU.add)
        vid3 = prog.emit("v", cmask2, inc=sB)
        prog.wait("v", sB, vid3)
        prog.barrier()
        def cpy(e):
            e.tensor_copy(moe_mb[:, 0:1], moe_sc[:, 0:1])
            e.tensor_copy(moe_mb[:, 1:2], moe_sc[:, 0:1])
            e.tensor_copy(moe_mb[:, 2:3], moe_sc[:, 1:2])
            return e.tensor_copy(moe_mb[:, 3:4], moe_sc[:, 1:2])
        vid4 = prog.emit("v", cpy, inc=sB)
        prog.wait("t", sB, vid4)
        prog.barrier()
        tid = prog.emit("t", lambda e: e.matmul(sb.big[:, 24:28], sb.onesb[:, 0:128],
                                                moe_mb[:, 0:4], start=True, stop=True),
                        inc=sBt)
        prog.wait("v", sBt, tid)
        def updm(e):
            e.tensor_scalar(moe_m32[:, 0:4], sb.big[:, 24:28], 1024.0, None,
                            op0=ALU.is_ge)
            return e.tensor_scalar(moe_m32[:, 4:8], sb.big[:, 24:28], 1024.0, None,
                                   op0=ALU.is_lt)
        prog.emit("v", updm, inc=sB)
        prog.wait("v", sB, sB.n)
        prog.barrier()
        def upd(e):
            e.copy_predicated(moe_st[:, 0:4], moe_m32[:, 0:4], moe_st[:, 8:12])
            return e.copy_predicated(moe_st[:, 4:8], moe_m32[:, 4:8], moe_st[:, 8:12])
        prog.emit("v", upd, inc=sB)
        prog.wait("v", sB, sB.n)
        prog.barrier()
    prog.barrier()
    notm = sbuf(prog, "moe_notm", [128, 256], U8)
    moe_pjh = sbuf(prog, "moe_pjh", [128, 24], BF16)
    moe_m32 = sbuf(prog, "moe_m32", [128, 8], I32)
    moe_pjx = sbuf(prog, "moe_pjx", [128, 128], BF16)
    cmp_dram = prog.nc.dram_tensor("cmp_bounce", [16, 1024], BF16, kind="Internal").ap()
    cmp32_dram = prog.nc.dram_tensor("cmp32_bounce", [3, 1024], F32, kind="Internal").ap()
    moe_pjf = sbuf(prog, "moe_pjf", [128, 24], F32)
    def masks(e):
        last = None
        for j in range(2):
            e.tensor_scalar(moe_u8[:, j * 128:(j + 1) * 128],
                            moe_lg[:, j * 128:(j + 1) * 128],
                            moe_st[:, 2 * j:2 * j + 1], None, op0=ALU.is_ge)
            e.tensor_scalar(notm[:, j * 128:(j + 1) * 128],
                            moe_lg[:, j * 128:(j + 1) * 128],
                            moe_st[:, 2 * j:2 * j + 1], None, op0=ALU.is_lt)
            last = e.tensor_scalar(moe_mb[:, j * 128:(j + 1) * 128],
                                   moe_lg[:, j * 128:(j + 1) * 128],
                                   moe_st[:, 2 * j:2 * j + 1], None, op0=ALU.is_ge)
        return last
    prog.emit("v", masks)
    prog.barrier()
    def iob(e):
        for c in range(8):
            e.tensor_scalar(sb.SCR[:, c * 128:(c + 1) * 128], sb.iota_i[:, :],
                            float(128 * c), None, op0=ALU.add)
        return e.memset(sb.SCR[:, 1280:1408], BIG_SLOT)
    prog.emit("v", iob)
    prog.barrier()
    def scans(e):
        e.tensor_tensor_scan(sb.SCR[:, 1024:1152].rearrange("p f -> p f"),
                             moe_mb[:, 0:128], sb.onesb[:, 0:128], 0.0,
                             op0=ALU.add, op1=ALU.mult)
        return e.tensor_tensor_scan(sb.SCR[:, 1152:1280], moe_mb[:, 128:256],
                                    sb.onesb[:, 0:128], 0.0, op0=ALU.add, op1=ALU.mult)
    prog.emit("v", scans)
    prog.barrier()
    def rsum(e):
        e.tensor_copy(moe_mb[:, 0:1], sb.SCR[:, 1151:1152])
        return e.tensor_copy(moe_mb[:, 1:2], sb.SCR[:, 1279:1280])
    prog.emit("v", rsum)
    prog.barrier()
    prog.emit("t", lambda e: e.matmul(sb.big[:, 26:28], sb.strictup[:, :],
                                      moe_mb[:, 0:2], start=True, stop=True))
    prog.barrier()
    prog.emit("v", lambda e: e.tensor_copy(moe_sc[:, 2:4], sb.big[:, 26:28]))
    prog.barrier()
    for j in range(2):
        def posj(e, j=j):
            e.tensor_scalar(sb.SCR[:, 1024 + j * 128:1152 + j * 128],
                            sb.SCR[:, 1024 + j * 128:1152 + j * 128],
                            moe_sc[:, 2 + j:3 + j], -1.0, op0=ALU.add, op1=ALU.add)
            return e.copy_predicated(sb.SCR[:, 1024 + j * 128:1152 + j * 128],
                                     notm[:, j * 128:(j + 1) * 128],
                                     sb.SCR[:, 1280:1408])
        prog.emit("v", posj)
    prog.barrier()
    return {"sc": moe_sc, "u8": moe_u8, "idx": moe_idx, "g": moe_g, "st": moe_st, "mb": moe_mb,
            "esel": moe_esel, "b1c": b1c, "b2c": b2c, "pjh": moe_pjh, "pjx": moe_pjx, "cmp_dram": cmp_dram, "cmp32": cmp32_dram, "pjf": moe_pjf}



def emit_moe_experts2(prog, sb, xn_full, lg_tok, fc1T_bf, fc2T_bf,
                      moe, y_out, ids_out):
    nc = prog.nc
    moe_sc = moe["sc"]; moe_u8 = moe["u8"]; moe_idx = moe["idx"]; moe_g = moe["g"]
    b1c = moe["b1c"]; b2c = moe["b2c"]
    ST = [sb.B1[6][:, 0:1024], sb.B1[7][:, 0:1024]]
    moe_pb = sbuf(prog, "moe_pb", [128, 256], BF16)
    for j in range(2):
        pos = sb.SCR[:, 1024 + j * 128:1152 + j * 128]
        def onz(e):
            e.memset(moe_pb[:, :], 0.0)
            e.tensor_copy(moe_pb[:, 0:1], sb.iota_colb[:, :])
            e.tensor_copy(moe_pb[:, 128:129], sb.iota_colb[:, :])
            e.memset(moe_pb[:, 2:3], 1.0)
            return e.memset(moe_pb[:, 130:131], 1.0)
        prog.emit("v", onz)
        prog.barrier()
        if j == 0:
            prog.emit("v", lambda e: e.tensor_copy(sb.SCR[:, 1888:1896],
                                                   sb.SCR[:, 1024:1032]))
            prog.emit("v", lambda e: e.tensor_copy(sb.SCR[:, 1896:1904],
                                                   moe["st"][:, 0:8]))
            prog.emit("v", lambda e: e.tensor_copy(sb.SCR[:, 1904:1912],
                                                   sb.SCR[:, 1024 + 44:1024 + 52]))
            prog.barrier()
        sSb = prog.sem("moe_stb")
        sSm = prog.sem("moe_stm")
        base_b = sSb.n
        base_m = sSm.n
        for j2 in range(128):
            if j2 >= 2:
                prog.wait("v", sSm, base_m + j2 - 1)
            cbase = 128 * (j2 % 2)
            def bld(e, j2=j2, j=j, cbase=cbase, pos=pos):
                e.tensor_scalar(ST[j2 % 2][:, :], sb.SCR[:, 0:1024],
                                pos[:, j2:j2 + 1], 0.0, op0=ALU.subtract, op1=ALU.is_equal)
                return e.memset(moe_pb[:, cbase + 1:cbase + 2], float(j2))
            bid = prog.emit("v", bld, inc=sSb)
            prog.wait("t", sSb, bid)
            def mmc(e, j2=j2, cbase=cbase):
                e.matmul(sb.big[:, 0:512], moe_pb[:, cbase:cbase + 128],
                         ST[j2 % 2][:, 0:512], start=(j2 == 0), stop=(j2 == 127))
                return e.matmul(sb.big[:, 512:1024], moe_pb[:, cbase:cbase + 128],
                                ST[j2 % 2][:, 512:1024], start=(j2 == 0), stop=(j2 == 127))
            prog.emit("t", mmc, inc=sSm)
            import os as _os
            if _os.environ.get("FORCE_SERIAL", "0") == "1":
                prog.barrier()
        prog.barrier()
        if j == 0:
            prog.emit("v", lambda e: e.tensor_copy(sb.SCR[0:3, 1912:1920],
                                                   sb.big[0:3, 484:492]))
            prog.emit("v", lambda e: e.tensor_copy(sb.SCR[0:3, 2024:2032],
                                                   sb.big[0:3, 0:8]))
            prog.barrier()
        prog.emit("v", lambda e: e.tensor_copy(sb.B1[4][:, 0:1024], sb.big[:, :]))
        prog.barrier()
        if j == 0:
            prog.emit("v", lambda e: e.tensor_copy(sb.SCR[0:3, 2032:2040],
                                                   sb.B1[4][0:3, 0:8]))
            prog.barrier()
        sT3 = prog.sem("p_tr")
        sE3 = prog.sem("p_evac")
        for c in range(8):
            tid = prog.emit("t", lambda e, c=c: e.transpose(
                sb.flexb[:, 0:128], sb.B1[4][:, c * 128:(c + 1) * 128],
                sb.ident_bf[:, :]), inc=sT3)
            prog.wait("v", sT3, tid)
            eid = prog.emit("v", lambda e, c=c: e.tensor_copy(
                moe["pjf"][:, c * 3:(c + 1) * 3], sb.flexb[:, 0:3]), inc=sE3)
            prog.wait("t", sE3, eid)
        prog.barrier()
        pjs = moe["pjf"][:].rearrange("p (c r) -> p c r", r=3)
        sA = prog.sem("moe_asm")
        a1 = prog.emit("v", lambda e: e.tensor_scalar(
            sb.SCR[:, 1456:1464], pjs[:, :, 0], 128.0, None, op0=ALU.mult), inc=sA)
        prog.wait("v", sA, a1)
        a2 = prog.emit("v", lambda e: e.tensor_tensor(
            sb.SCR[:, 1456:1464], sb.SCR[:, 1456:1464], pjs[:, :, 1], op=ALU.add), inc=sA)
        prog.wait("v", sA, a2)
        a3 = prog.emit("v", lambda e: e.tensor_copy(moe_sc[:, 8:16], pjs[:, :, 2]), inc=sA)
        prog.wait("v", sA, a3)
        prog.barrier()
        prog.emit("v", lambda e: e.tensor_copy(moe_sc[:, 0:8], sb.SCR[:, 1456:1464]))
        prog.barrier()
        def toidx(e, j=j):
            return e.tensor_copy(moe_idx[:, j * 8:(j + 1) * 8], moe_sc[:, 0:8])
        prog.emit("v", toidx)
        prog.barrier()
        prog.dma("g", ids_out[j], moe_sc[:, 0:8])
        prog.barrier()
        sG = prog.sem("moe_g1")
        for sc in range(8):
            if sc >= 1:
                prog.wait("g", prog.sem("moe_g5"), prog.sem("moe_g5").n)
            gid = prog.idma(out=sb.SCR[:, 1408:1424],
                            out_offset=None, in_=lg_tok,
                            in_offset=IndirectOffsetOnAxis(moe_idx[:, j * 8 + sc:j * 8 + sc + 1], 0),
                            bounds_check=16383, oob_is_err=False, sem=sG)
            prog.wait("v", sG, gid)
            prog.barrier()
            v0 = prog.emit("v", lambda e, j=j: e.tensor_mul(
                sb.SCR[:, 1440:1456], sb.SCR[:, 1408:1424],
                moe["esel"][:, j * 16:(j + 1) * 16]), inc=prog.sem("moe_g2"))
            prog.wait("v", prog.sem("moe_g2"), v0)
            prog.barrier()
            v1 = prog.emit("v", lambda e: e.reduce_sum(
                moe_sc[:, 4:5], sb.SCR[:, 1440:1456], axis=AX.X),
                inc=prog.sem("moe_g2"))
            prog.wait("v", prog.sem("moe_g2"), v1)
            prog.barrier()
            def gsub(e):
                return e.tensor_scalar(sb.SCR[:, 1424:1440], sb.SCR[:, 1408:1424],
                                       moe_sc[:, 4:5], None, op0=ALU.subtract)
            v2 = prog.emit("v", gsub, inc=prog.sem("moe_g2"))
            prog.wait("s", prog.sem("moe_g2"), v2)
            prog.barrier()
            sid = prog.emit("s", lambda e: e.activation(
                sb.SCR[:, 1440:1456], sb.SCR[:, 1424:1440], AF.Exp),
                inc=prog.sem("moe_g3"))
            prog.wait("v", prog.sem("moe_g3"), sid)
            prog.barrier()
            v3 = prog.emit("v", lambda e: e.reduce_sum(moe_sc[:, 6:7],
                                                       sb.SCR[:, 1440:1456], axis=AX.X),
                           inc=prog.sem("moe_g4"))
            prog.wait("v", prog.sem("moe_g4"), v3)
            prog.barrier()
            v4 = prog.emit("v", lambda e, j=j, sc=sc: e.reciprocal(
                moe_g[:, j * 8 + sc:j * 8 + sc + 1], moe_sc[:, 6:7]),
                inc=prog.sem("moe_g5"))
            prog.wait("v", prog.sem("moe_g5"), v4)
            prog.barrier()
        prog.barrier()
        sX = prog.sem("moe_xe")
        def xe_slot(sc):
            return sb.B2[sc // 2][:, (sc % 2) * 1024:(sc % 2 + 1) * 1024]
        def xeT(k):
            return sb.B2[4 + k // 2][:, (k % 2) * 1024:(k % 2 + 1) * 1024]
        for sc in range(8):
            prog.idma(out=xe_slot(sc), out_offset=None, in_=xn_full,
                      in_offset=IndirectOffsetOnAxis(moe_idx[:, j * 8 + sc:j * 8 + sc + 1], 0),
                      bounds_check=16383, oob_is_err=False, sem=sX)
        prog.barrier()
        sT = prog.sem("p_tr")
        sE = prog.sem("p_evac")
        tr_done = {}
        ev_done = {}
        for sc in range(8):
            if sc >= 1:
                prog.wait("t", sE, ev_done[sc - 1])
            def trg(e, sc=sc):
                last = None
                for k in range(NCH):
                    last = e.transpose(sb.flexb[:, k * 128:(k + 1) * 128],
                                       xe_slot(sc)[:, k * 128:(k + 1) * 128],
                                       sb.ident_bf[:, :])
                return last
            tr_done[sc] = prog.emit("t", trg, inc=sT)
            prog.wait("v", sT, tr_done[sc])
            def evg(e, sc=sc):
                last = None
                for k in range(NCH):
                    last = e.tensor_copy(xeT(k)[:, sc * 128:(sc + 1) * 128],
                                         sb.flexb[:, k * 128:(k + 1) * 128])
                return last
            ev_done[sc] = prog.emit("v", evg, inc=sE)
        prog.barrier()
        def pad0(e):
            e.memset(sb.B1[2][:, 1024:2048], 0.0)
            return e.memset(sb.B1[3][:, 1024:2048], 0.0)
        prog.emit("v", pad0)
        prog.barrier()
        def hslot(fc):
            return sb.B3[fc // 4][:, (fc % 4) * 512:(fc % 4 + 1) * 512]
        def yTslot(k):
            return sb.B1[k // 4][:, (k % 4) * 512:(k % 4 + 1) * 512]
        for hh in range(2):
            emit_proj_fm(prog, sb, [xeT(k) for k in range(NCH)], fc1T_bf[j],
                         None, 1024, 4096, evac_eng="s", n_slices=1, sl_off=hh * 512,
                         bias_cols=[b1c[:, j * 32 + fc:j * 32 + fc + 1] for fc in range(32)],
                         act=AF.Gelu, out_tiles=[hslot(fc) for fc in range(32)])
            def yevac(e, fc, ns, ps, j=j):
                return e.tensor_scalar(yTslot(fc), ps[:, :],
                                       b2c[:, j * 8 + fc:j * 8 + fc + 1], None, op0=ALU.add)
            emit_proj_fm(prog, sb, [hslot(fc) for fc in range(32)], fc2T_bf[j],
                         yevac, 4096, 1024, evac_eng="v", n_slices=1, sl_off=0)
            sT2 = prog.sem("p_tr")
            sE2 = prog.sem("p_evac")
            sY = [prog.sem("moe_y0"), prog.sem("moe_y1")]
            trd = {}
            evd = {}
            ystores = {}
            for t4 in range(4):
                sc = hh * 4 + t4
                if t4 >= 1:
                    prog.wait("t", sE2, evd[t4 - 1])
                def trg2(e, t4=t4):
                    last = None
                    for k in range(NCH):
                        last = e.transpose(sb.flexb[:, k * 128:(k + 1) * 128],
                                           yTslot(k)[:, t4 * 128:(t4 + 1) * 128],
                                           sb.ident_bf[:, :])
                    return last
                trd[t4] = prog.emit("t", trg2, inc=sT2)
                prog.wait("v", sT2, trd[t4])
                if t4 >= 2:
                    prog.wait("v", sY[t4 % 2], ystores[t4 - 2])
                ytok = sb.B1[2 + t4 % 2][:, :]
                def evg2(e, t4=t4, sc=sc, j=j, ytok=ytok):
                    e.tensor_copy(ytok[:, 0:1024], sb.flexb[:, :])
                    e.tensor_scalar(ytok[:, 0:1024], ytok[:, 0:1024],
                                    moe_g[:, j * 8 + sc:j * 8 + sc + 1], None, op0=ALU.mult)
                    return e.tensor_copy(ytok[:, 1024:1025],
                                         moe_g[:, j * 8 + sc:j * 8 + sc + 1])
                evd[t4] = prog.emit("v", evg2, inc=sE2)
                prog.wait("y", sE2, evd[t4])
                ystores[t4] = prog.dma("y", y_out[j][sc * 128:(sc + 1) * 128, :],
                                       ytok[:, 0:1040], sem=sY[t4 % 2])
            prog.barrier()
    prog.barrier()


def build_launch_experts(debug=False):
    nc = bass.Bass("TRN2", target_bir_lowering=False, detect_race_conditions=False)
    xn_d = nc.dram_tensor("xn", [16384, 1024], BF16, kind="ExternalInput").ap()
    lgmy_d = nc.dram_tensor("lg_my", [2, 16384], F32, kind="ExternalInput").ap()
    lgtok_d = nc.dram_tensor("lg_tok", [16384, 16], F32, kind="ExternalInput").ap()
    esel_d = nc.dram_tensor("esel", [2, 128, 16], F32, kind="ExternalInput").ap()
    fc1_d = nc.dram_tensor("fc1", [2, 4096, 1024], F32, kind="ExternalInput").ap()
    fc1b_d = nc.dram_tensor("fc1b", [2, 4096], F32, kind="ExternalInput").ap()
    fc2_d = nc.dram_tensor("fc2", [2, 1024, 4096], F32, kind="ExternalInput").ap()
    fc2b_d = nc.dram_tensor("fc2b", [2, 1024], F32, kind="ExternalInput").ap()
    y_o = nc.dram_tensor("y", [2, 1024, 1040], BF16, kind="ExternalOutput").ap()
    ids_o = nc.dram_tensor("ids", [2, 128, 8], F32, kind="ExternalOutput").ap()
    fc1bf_i = nc.dram_tensor("fc1bf", [2, 4096, 1024], BF16, kind="Internal").ap()
    fc2bf_i = nc.dram_tensor("fc2bf", [2, 1024, 4096], BF16, kind="Internal").ap()
    fc1T_i = nc.dram_tensor("fc1T", [2, 1024, 4096], BF16, kind="Internal").ap()
    fc2T_i = nc.dram_tensor("fc2T", [2, 4096, 1024], BF16, kind="Internal").ap()
    prog = Prog(nc)
    sb = alloc_common(prog)
    emit_init_consts(prog, sb)
    if debug:
        prog.emit("v", lambda e: e.memset(sb.SCR[:, 1792:2048], 0.0))
        prog.barrier()
    emit_moe_weight_prep(prog, sb, fc1_d, fc2_d,
                         [fc1T_i[0], fc1T_i[1]], [fc2T_i[0], fc2T_i[1]],
                         ([fc1bf_i[0], fc1bf_i[1]], [fc2bf_i[0], fc2bf_i[1]]))
    moe = emit_moe_experts(prog, sb, lgmy_d, esel_d, [fc1b_d[0], fc1b_d[1]],
                           [fc2b_d[0], fc2b_d[1]])
    emit_moe_experts2(prog, sb, xn_d, lgtok_d, [fc1T_i[0], fc1T_i[1]],
                      [fc2T_i[0], fc2T_i[1]], moe, [y_o[0], y_o[1]],
                      [ids_o[0], ids_o[1]])
    if debug:
        dbg_o = nc.dram_tensor("dbg", [128, 256], F32, kind="ExternalOutput").ap()
        def dmp(e):
            e.memset(sb.SCR[:, 2046:2048], 777.0)
            e.tensor_copy(sb.SCR[:, 1792:1808], moe["st"][:, :])
            e.tensor_copy(sb.SCR[:, 1808:1824], moe["g"][:, :])
            e.tensor_copy(sb.SCR[:, 1824:1832], sb.SCR[:, 1456:1464])
            e.tensor_copy(sb.SCR[:, 1832:1848], moe["idx"][:, :])
            return e.tensor_copy(sb.SCR[:, 1848:1856], sb.SCR[:, 1024:1032])
        prog.emit("v", dmp)
        prog.barrier()
        prog.dma("g", dbg_o, sb.SCR[:, 1792:2048])
        prog.barrier()
    prog.finalize()
    return nc


def emit_moe_combine(prog, sb, y_all, ids_all, mybase_col, acc_dram, lid16_dram,
                     scale=0.5):
    nc = prog.nc
    I16 = mybir.dt.int16
    cidx = sb.SCR[:, :].bitcast(I32)[:, 1424:1552]
    di = sb.SCR[:, :].bitcast(I32)[:, 1552:1560]
    ci16 = sb.SCR[:, :].bitcast(I16)[:, 3200:3328]
    idxw = sb.rep[:, :].bitcast(I16)[:, 0:1024]
    cids = sb.SCR[:, 1040:1168]
    ctmp = sb.SCR[:, 1168:1296]
    dumpf = sb.SCR[:, 1296:1424]
    for e in range(16):
        prog.dma("y", cids[:, e * 8:(e + 1) * 8], ids_all[e])
    prog.emit("g", lambda e: e.iota(di[:, :], pattern=[[128, 8]], base=2048,
                                    channel_multiplier=1))
    prog.barrier()
    def cpd(e):
        last = None
        for ee in range(16):
            last = e.tensor_copy(dumpf[:, ee * 8:(ee + 1) * 8], di[:, :])
        return last
    prog.emit("v", cpd)
    prog.barrier()
    prog.emit("v", lambda e: e.tensor_scalar(cids[:, :], cids[:, :], mybase_col,
                                             None, op0=ALU.subtract))
    prog.barrier()
    prog.emit("v", lambda e: e.tensor_scalar(ctmp[:, :], cids[:, :], 0.0, None,
                                             op0=ALU.is_lt))
    prog.barrier()
    prog.emit("v", lambda e: e.tensor_scalar(sb.SCR[:, 0:128], cids[:, :], 2047.5,
                                             None, op0=ALU.is_ge))
    prog.barrier()
    prog.emit("v", lambda e: e.tensor_add(ctmp[:, :], ctmp[:, :], sb.SCR[:, 0:128]))
    prog.barrier()
    prog.emit("v", lambda e: e.tensor_copy(cidx[:, :], ctmp[:, :]))
    prog.barrier()
    prog.emit("v", lambda e: e.copy_predicated(cids[:, :], cidx[:, :], dumpf[:, :]))
    prog.barrier()
    prog.emit("v", lambda e: e.tensor_copy(cidx[:, :], cids[:, :]))
    prog.barrier()
    prog.emit("v", lambda e: e.tensor_copy(ci16[:, :], cidx[:, :]))
    prog.barrier()
    prog.dma("y", lid16_dram, ci16[:, :])
    prog.barrier()
    for e in range(16):
        src_e = lid16_dram[:, 8 * e:8 * e + 8].rearrange("(q cc) g -> q cc g", q=16)
        for k in range(8):
            prog.dma("y", idxw[16 * k:16 * (k + 1), e * 64:(e + 1) * 64].rearrange(
                "q (cc g) -> q cc g", cc=8), src_e)
    prog.emit("v", lambda e: e.memset(sb.B1[0][:, :], 0.0))
    prog.emit("v", lambda e: e.memset(sb.wbuf[:, :], 0.0))
    prog.barrier()
    sZ = prog.sem("cmb_z")
    for tt in range(NTT):
        prog.dma("y", acc_dram[tt * 128:(tt + 1) * 128, :], sb.B1[0][:, 0:1152],
                 sem=sZ)
    prog.barrier()
    prog.emit("g", lambda e: e.load_library(mlp_lib))
    sLd = prog.sem("cmb_l0")
    sSc = prog.sem("cmb_s")
    for e in range(16):
        for h in range(2):
            prog.wait("y", sSc, sSc.n)
            for c2 in range(4):
                prog.dma("y",
                         sb.wbuf[:, c2 * 1152:c2 * 1152 + 1040].rearrange(
                             "p (u f) -> p u f", u=1),
                         y_all[e].rearrange("(p s) f -> p s f", s=8)[
                             :, 4 * h + c2:4 * h + c2 + 1, :],
                         sem=sLd)
            prog.wait("g", sLd, sLd.n)
            def sct(eng, e=e, h=h):
                return eng.dma_scatter_add(
                    acc_dram,
                    sb.wbuf[:, 0:4608].rearrange("p (c f) -> p c f", f=1152),
                    idxw[:, e * 64 + h * 32:e * 64 + (h + 1) * 32],
                    512, 512, 1152, elem_step=1152)
            prog.emit("g", sct, inc=sSc, k=16)
    prog.wait("g", sSc, sSc.n)
    prog.emit("g", lambda e: e.load_library(std_lib))
    prog.barrier()
    sA = prog.sem("cmb_a")
    def slot(tt):
        return sb.B3[tt // 2][:, (tt % 2) * 1024:(tt % 2 + 1) * 1024]
    for tt in range(NTT):
        ld = prog.dma("y", sb.B1[0][:, 0:1152], acc_dram[tt * 128:(tt + 1) * 128, :],
                      sem=sA)
        prog.wait("v", sA, ld)
        prog.barrier()
        prog.emit("v", lambda e: e.tensor_copy(sb.SCR[:, 0:1026], sb.B1[0][:, 0:1026]))
        prog.barrier()
        prog.emit("v", lambda e: e.tensor_scalar(ctmp[:, 0:2], sb.SCR[:, 1024:1026],
                                                 1e-30, None, op0=ALU.max))
        prog.barrier()
        prog.emit("v", lambda e: e.reciprocal(ctmp[:, 2:4], ctmp[:, 0:2]))
        prog.barrier()
        prog.emit("v", lambda e, tt=tt: e.tensor_scalar(
            slot(tt), sb.SCR[:, 0:1024], ctmp[:, 2:3], None, op0=ALU.mult))
        prog.barrier()
    emit_tm_to_fm(prog, sb, slot, sb.xT, add_scale=scale)


def emit_cast_weights(prog, sb, pairs):
    for (src, dst) in pairs:
        prog.dma("g", dst, src)
    prog.barrier()


def emit_l0_attn(prog, sb, wq_bf, wk_bf, wv_bf, wproj_bf, w1_bf, w2_bf,
                 ln1g, ln1b, bq, bk, bv, bproj, ln2g, ln2b, b1, b2,
                 u_dram, qT_dram, ctx_dram, dbg=None):
    nc = prog.nc
    b1c = sbuf(prog, "l0_b1c", [128, 32], F32)
    for i, vec in enumerate((ln1g, ln1b, bq, bk, bv, bproj, ln2g, ln2b, b2)):
        emit_load_perch(prog, sb, sb.pp[:, i * 8:(i + 1) * 8], vec)
    emit_load_perch(prog, sb, b1c[:, :], b1)
    prog.barrier()
    emit_ln(prog, sb, sb.xT, sb.B1)
    for k in range(NCH):
        prog.dma("y", u_dram[k * 128:(k + 1) * 128, :], sb.B1[k][:, :])
    prog.barrier()
    emit_ln(prog, sb, sb.B1, sb.B2,
            gamma=[sb.pp[:, 0 * 8 + k:0 * 8 + k + 1] for k in range(NCH)],
            beta=[sb.pp[:, 1 * 8 + k:1 * 8 + k + 1] for k in range(NCH)])
    def evq(e, fc, ns, ps):
        return e.tensor_scalar(sb.B3[fc][:, ns * 512:(ns + 1) * 512], ps[:, :],
                               sb.pp[:, 16 + fc:17 + fc], 0.125, op0=ALU.add,
                               op1=ALU.mult)
    emit_proj_fm(prog, sb, sb.B2, wq_bf, evq, 1024, 1024)
    for k in range(NCH):
        prog.dma("y", qT_dram[k * 128:(k + 1) * 128, :], sb.B3[k][:, :])
    prog.barrier()
    def evk(e, fc, ns, ps):
        return e.tensor_scalar(sb.B3[fc][:, ns * 512:(ns + 1) * 512], ps[:, :],
                               sb.pp[:, 24 + fc:25 + fc], None, op0=ALU.add)
    emit_proj_fm(prog, sb, sb.B2, wk_bf, evk, 1024, 1024)
    def evv(e, fc, ns, ps):
        return e.tensor_scalar(sb.B1[fc][:, ns * 512:(ns + 1) * 512], ps[:, :],
                               sb.pp[:, 32 + fc:33 + fc], None, op0=ALU.add)
    emit_proj_fm(prog, sb, sb.B2, wv_bf, evv, 1024, 1024)
    def vslot(tt):
        return sb.B2[tt // 2][:, (tt % 2) * 1024:(tt % 2 + 1) * 1024]
    emit_fm_to_tm(prog, sb, sb.B1, vslot)
    acol = sbuf(prog, "l0_acol", [128, 4], F32)
    def mka(e):
        e.tensor_scalar(acol[:, 0:1], sb.iota_colf[:, :], 1.0 / 255.0, None, op0=ALU.mult)
        e.tensor_scalar(acol[:, 1:2], sb.iota_colf[:, :], 1.0 / 255.0, 128.0 / 255.0,
                        op0=ALU.mult, op1=ALU.add)
        e.tensor_scalar(acol[:, 2:3], sb.iota_colf[:, :], -1.0 / 255.0, 1.0,
                        op0=ALU.mult, op1=ALU.add)
        return e.tensor_scalar(acol[:, 3:4], sb.iota_colf[:, :], -1.0 / 255.0,
                               1.0 - 128.0 / 255.0, op0=ALU.mult, op1=ALU.add)
    prog.emit("v", mka)
    prog.barrier()
    CH = [(max(0, i - 128), min(T, i + 384), i) for i in range(0, T, 256)]
    smC = sbuf(prog, "l0_smc", [128, 16], F32)
    sQ = prog.sem("l0_q")
    sPs = prog.sem("l0_ps")
    for ci, (s0, e0, i0) in enumerate(CH):
        span = e0 - s0
        nt4 = span // 128
        for k in range(NCH):
            prog.dma("y", sb.wbuf[:, k * 512:k * 512 + span],
                     qT_dram[k * 128:(k + 1) * 128, s0:e0], sem=sQ)
        prog.barrier()
        for h in range(16):
            kc, ko = h // 2, (h % 2) * 64
            for t4 in range(nt4):
                ps = sb.mm[t4 % 2]
                def mmS(e, kc=kc, ko=ko, t4=t4, ps=ps, s0=s0, span=span):
                    return e.matmul(ps[:, 0:span],
                                    sb.wbuf[ko:ko + 64, kc * 512 + t4 * 128:kc * 512 + (t4 + 1) * 128],
                                    sb.B3[kc][ko:ko + 64, s0:s0 + span],
                                    start=True, stop=True)
                mid = prog.emit("t", mmS, inc=sPs)
                prog.wait("v", sPs, mid)
                prog.barrier()
                prog.emit("v", lambda e, ps=ps, span=span: e.reduce_max(
                    smC[:, 0:1], ps[:, 0:span], axis=AX.X, negate=True))
                prog.barrier()
                sid = prog.emit("s", lambda e, ps=ps, span=span: e.activation(
                    sb.rep[:, 0:span], ps[:, 0:span], AF.Exp, bias=smC[:, 0:1]),
                    inc=prog.sem("l0_e"))
                prog.wait("v", prog.sem("l0_e"), sid)
                prog.barrier()
                prog.emit("v", lambda e, span=span: e.reduce_sum(
                    smC[:, 2:3], sb.rep[:, 0:span], axis=AX.X))
                prog.barrier()
                prog.emit("v", lambda e: e.reciprocal(smC[:, 4:5], smC[:, 2:3]))
                prog.barrier()
                prog.emit("v", lambda e, span=span: e.tensor_scalar(
                    sb.rep[:, 0:span], sb.rep[:, 0:span], smC[:, 4:5], None,
                    op0=ALU.mult))
                prog.barrier()
                nkt = span // 128
                for kt in range(nkt):
                    tid = prog.emit("t", lambda e, kt=kt: e.transpose(
                        sb.flexb[:, 0:128], sb.rep[:, kt * 128:(kt + 1) * 128].bitcast(BF16)[:, 0:128]
                        if False else sb.rep[:, kt * 128:(kt + 1) * 128],
                        sb.ident_bf[:, :]), inc=prog.sem("l0_t"))
                    prog.wait("v", prog.sem("l0_t"), tid)
                    eid = prog.emit("v", lambda e, kt=kt: e.tensor_copy(
                        sb.repQ[:, kt * 128:(kt + 1) * 128], sb.flexb[:, 0:128]),
                        inc=prog.sem("l0_t2"))
                    prog.wait("t", prog.sem("l0_t2"), eid)
                prog.barrier()
                def mmAV(e, nkt=nkt, s0=s0, h=h):
                    last = None
                    for kt in range(nkt):
                        gt = (s0 + kt * 128) // 128
                        last = e.matmul(sb.big[:, 0:64],
                                        sb.repQ[:, kt * 128:(kt + 1) * 128],
                                        vslot(gt)[:, h * 64:(h + 1) * 64],
                                        start=(kt == 0), stop=(kt == nkt - 1))
                    return last
                aid = prog.emit("t", mmAV, inc=prog.sem("l0_av"))
                prog.wait("v", prog.sem("l0_av"), aid)
                prog.barrier()
                prog.emit("v", lambda e, t4=t4, h=h: e.tensor_copy(
                    sb.B1[t4][:, h * 64:(h + 1) * 64], sb.big[:, 0:64]))
                prog.barrier()
        for t4 in range(nt4):
            row0 = s0 + t4 * 128
            if ci > 0 and t4 < 2:
                ld = prog.dma("y", sb.rep[:, 0:1024], ctx_dram[row0:row0 + 128, :],
                              sem=prog.sem("l0_cb"))
                prog.wait("v", prog.sem("l0_cb"), ld)
                prog.barrier()
                prog.emit("v", lambda e, t4=t4: e.tensor_scalar(
                    sb.SCR[:, 0:1024], sb.B1[t4][:, 0:1024], acol[:, t4:t4 + 1],
                    None, op0=ALU.mult))
                prog.barrier()
                prog.emit("v", lambda e, t4=t4: e.scalar_tensor_tensor(
                    sb.SCR[:, 0:1024], sb.rep[:, 0:1024], acol[:, 2 + t4:3 + t4],
                    sb.SCR[:, 0:1024], op0=ALU.mult, op1=ALU.add))
                prog.barrier()
                prog.emit("v", lambda e: e.tensor_copy(sb.rep[:, 0:1024].bitcast(BF16)[:, 0:1024]
                                                       if False else sb.rep[:, 0:1024],
                                                       sb.SCR[:, 0:1024]))
                prog.barrier()
                st = prog.dma("y", ctx_dram[row0:row0 + 128, :], sb.rep[:, 0:1024],
                              sem=prog.sem("l0_cb"))
                prog.wait("v", prog.sem("l0_cb"), st)
            else:
                prog.emit("v", lambda e, t4=t4: e.tensor_copy(
                    sb.rep[:, 0:1024], sb.B1[t4][:, 0:1024]))
                prog.barrier()
                st = prog.dma("y", ctx_dram[row0:row0 + 128, :], sb.rep[:, 0:1024],
                              sem=prog.sem("l0_cb"))
                prog.wait("v", prog.sem("l0_cb"), st)
            prog.barrier()
    prog.barrier()
    def cslot(tt):
        return sb.B3[tt // 2][:, (tt % 2) * 1024:(tt % 2 + 1) * 1024]
    sC = prog.sem("l0_c2")
    for tt in range(NTT):
        prog.dma("y", cslot(tt), ctx_dram[tt * 128:(tt + 1) * 128, :], sem=sC)
    prog.barrier()
    emit_tm_to_fm(prog, sb, cslot, sb.B2)
    for k in range(NCH):
        prog.dma("y", sb.B1[k][:, :], u_dram[k * 128:(k + 1) * 128, :])
    prog.barrier()
    if dbg is not None:
        for k in range(NCH):
            prog.dma("y", dbg[2][k * 128:(k + 1) * 128, :], sb.B2[k][:, :])
            prog.dma("y", dbg[3][k * 128:(k + 1) * 128, :], sb.B1[k][:, :])
        prog.barrier()
    def evp(e, fc, ns, ps):
        sl = slice(ns * 512, (ns + 1) * 512)
        e.tensor_scalar(sb.SCR[:, 0:512], ps[:, :], sb.pp[:, 40 + fc:41 + fc],
                        None, op0=ALU.add)
        return e.tensor_add(sb.B3[fc][:, sl], sb.SCR[:, 0:512], sb.B1[fc][:, sl])
    emit_proj_fm(prog, sb, sb.B2, wproj_bf, evp, 1024, 1024)
    if dbg is not None:
        for k in range(NCH):
            prog.dma("y", dbg[0][k * 128:(k + 1) * 128, :], sb.B3[k][:, :])
        prog.barrier()
    emit_ln(prog, sb, sb.B3, sb.B2,
            gamma=[sb.pp[:, 48 + k:49 + k] for k in range(NCH)],
            beta=[sb.pp[:, 56 + k:57 + k] for k in range(NCH)])
    if dbg is not None:
        for k in range(NCH):
            prog.dma("y", dbg[1][k * 128:(k + 1) * 128, :], sb.B2[k][:, :])
        prog.barrier()
    for tc in range(4):
        def h1s(fc):
            return sb.B1[fc // 4][:, (fc % 4) * 512:(fc % 4 + 1) * 512]
        emit_proj_fm(prog, sb, sb.B2, w1_bf, None, 1024, 4096, evac_eng="s",
                     n_slices=1, sl_off=tc * 512,
                     bias_cols=[b1c[:, fc:fc + 1] for fc in range(32)],
                     act=AF.Gelu, out_tiles=[h1s(fc) for fc in range(32)])
        def evo(e, fc, ns, ps, tc=tc):
            sl = slice(tc * 512, (tc + 1) * 512)
            e.tensor_scalar(sb.SCR[:, 0:512], ps[:, :], sb.pp[:, 64 + fc:65 + fc],
                            None, op0=ALU.add)
            e.tensor_add(sb.SCR[:, 0:512], sb.SCR[:, 0:512], sb.B3[fc][:, sl])
            return e.tensor_add(sb.xT[fc][:, sl], sb.xT[fc][:, sl], sb.SCR[:, 0:512])
        emit_proj_fm(prog, sb, [h1s(fc) for fc in range(32)], w2_bf, evo, 4096, 1024,
                     n_slices=1, sl_off=0)
    prog.barrier()


def emit_l1ffn_prep(prog, sb, k0_u8, k1_u8, k2_u8, w0p, w1p, w2p):
    u8v = sb.SCR[:, :].bitcast(U8)
    sL = prog.sem("fp_l")
    sS = prog.sem("fp_s")
    for kc in range(8):
        ld1 = prog.dma("y", u8v[:, 0:4096], k0_u8[kc * 128:(kc + 1) * 128, :], sem=sL)
        ld2 = prog.dma("y", u8v[:, 4096:8192], k0_u8[1024 + kc * 128:1024 + (kc + 1) * 128, :], sem=sL)
        prog.wait("v", sL, ld2)
        prog.barrier()
        for hh in range(2):
            sl = slice(hh * 2048, (hh + 1) * 2048)
            def cv(e, hh=hh):
                e.tensor_copy(sb.rep[:, :], u8v[:, hh * 2048:(hh + 1) * 2048])
                return e.tensor_copy(sb.repQ[:, :],
                                     u8v[:, 4096 + hh * 2048:4096 + (hh + 1) * 2048])
            prog.emit("v", cv)
            prog.barrier()
            prog.emit("v", lambda e: e.tensor_tensor(sb.rep[:, :], sb.rep[:, :],
                                                     sb.repQ[:, :], op=ALU.subtract))
            prog.barrier()
            s1 = prog.dma("y", w0p[kc * 128:(kc + 1) * 128, sl], sb.rep[:, :], sem=sS)
            prog.wait("v", sS, s1)
            prog.barrier()
    for (ksrc, wdst, FF, CC) in ((k1_u8, w1p, 4096, 32), (k2_u8, w2p, 1024, 32)):
        for kc in range(CC):
            ld = prog.dma("y", u8v[:, 0:FF], ksrc[kc * 128:(kc + 1) * 128, :], sem=sL)
            prog.wait("v", sL, ld)
            prog.barrier()
            def cv2(e, FF=FF):
                last = e.tensor_copy(sb.rep[:, 0:min(FF, 2048)], u8v[:, 0:min(FF, 2048)])
                if FF > 2048:
                    last = e.tensor_copy(sb.repQ[:, 0:FF - 2048], u8v[:, 2048:FF])
                return last
            prog.emit("v", cv2)
            prog.barrier()
            def aff(e, FF=FF):
                last = e.tensor_scalar(sb.rep[:, 0:min(FF, 2048)],
                                       sb.rep[:, 0:min(FF, 2048)],
                                       2.0, -1.0, op0=ALU.mult, op1=ALU.add)
                if FF > 2048:
                    last = e.tensor_scalar(sb.repQ[:, 0:FF - 2048],
                                           sb.repQ[:, 0:FF - 2048],
                                           2.0, -1.0, op0=ALU.mult, op1=ALU.add)
                return last
            prog.emit("v", aff)
            prog.barrier()
            s1 = prog.dma("y", wdst[kc * 128:(kc + 1) * 128, 0:min(FF, 2048)],
                          sb.rep[:, 0:min(FF, 2048)], sem=sS)
            if FF > 2048:
                s1 = prog.dma("y", wdst[kc * 128:(kc + 1) * 128, 2048:FF],
                              sb.repQ[:, 0:FF - 2048], sem=sS)
            prog.wait("v", sS, s1)
            prog.barrier()
    prog.barrier()


def emit_l1_ffn(prog, sb, w0p, w1p, w2p, scol, dbg_h=None):
    nc = prog.nc
    if not hasattr(sb, "l1_dcol"):
        sb.l1_dcol = sbuf(prog, "l1_dcol", [128, 8], F32)
    dcol = sb.l1_dcol
    prog.emit("v", lambda e: e.tensor_scalar(dcol[:, 0:6], scol[:, 0:6], 2.0, None,
                                             op0=ALU.mult))
    prog.barrier()
    prog.emit("v", lambda e: e.tensor_scalar(dcol[:, 0:2], dcol[:, 0:2], 2.0, None,
                                             op0=ALU.mult))
    prog.barrier()
    prog.emit("v", lambda e: e.tensor_scalar(dcol[:, 6:8], scol[:, 4:6], 0.5, None,
                                             op0=ALU.mult))
    prog.barrier()
    emit_ln(prog, sb, sb.xT, sb.B2)
    for tc in range(4):
        def h1s(fc):
            return sb.B1[fc // 4][:, (fc % 4) * 512:(fc % 4 + 1) * 512]
        def h2s(fc):
            return sb.B3[fc // 4][:, (fc % 4) * 512:(fc % 4 + 1) * 512]
        def ev1(e, fc, ns, ps):
            e.tensor_scalar(sb.SCR[:, 0:512], ps[:, :], dcol[:, 0:1], None, op0=ALU.mult)
            return e.tensor_relu(h1s(fc), sb.SCR[:, 0:512])
        emit_proj_fm(prog, sb, sb.B2, w0p, ev1, 1024, 4096, n_slices=1, sl_off=tc * 512)
        if dbg_h is not None and tc == 0:
            for fc in range(32):
                prog.dma("y", dbg_h[fc * 128:(fc + 1) * 128, :], h1s(fc))
            prog.barrier()
        def ev2(e, fc, ns, ps):
            e.tensor_scalar(sb.SCR[:, 0:512], ps[:, :], dcol[:, 2:3], None, op0=ALU.mult)
            return e.tensor_relu(h2s(fc), sb.SCR[:, 0:512])
        emit_proj_fm(prog, sb, [h1s(fc) for fc in range(32)], w1p, ev2, 4096, 4096,
                     n_slices=1, sl_off=0)
        def ev3(e, fc, ns, ps, tc=tc):
            sl = slice(tc * 512, (tc + 1) * 512)
            return e.scalar_tensor_tensor(sb.xT[fc][:, sl], ps[:, :], dcol[:, 6:7],
                                          sb.xT[fc][:, sl], op0=ALU.mult, op1=ALU.add)
        emit_proj_fm(prog, sb, [h2s(fc) for fc in range(32)], w2p, ev3, 4096, 1024,
                     n_slices=1, sl_off=0)
    prog.barrier()


def emit_l1_attn(prog, sb, wq_bf, wk_bf, wv_bf, wo_bf, wg_bf, wf_bf, bf_dram,
                 qnw, knw, onw_rep_dram, kcw, vcw, q_aug_dram, kaug_dram, g_dram):
    nc = prog.nc
    EPS = 1e-6
    qnw_c = sbuf(prog, "l1_qnw", [128, 8], F32)
    knw_c = sbuf(prog, "l1_knw", [128, 8], F32)
    bf_c = sbuf(prog, "l1_bf", [16, 1], F32)
    cw = sbuf(prog, "l1_cw", [128, 64], F32)
    wfc = sbuf(prog, "l1_wfc", [128, 128], BF16)
    qa = sbuf(prog, "l1_qa", [66, 128], BF16)
    kbuf = sb.B1[7][0:66, :]
    smC = sbuf(prog, "l1_smc", [128, 16], F32)
    dmask = sbuf(prog, "l1_dm", [128, 128], F32)
    cum_dram = nc.dram_tensor("l1_cum", [32, T], BF16, kind="Internal").ap()
    cumhi = sb.B1[6][0:16, 0:T]
    cumlo = sb.B1[7][0:16, 0:T]
    emit_load_perch(prog, sb, qnw_c[:, :], qnw)
    emit_load_perch(prog, sb, knw_c[:, :], knw)
    prog.dma("g", bf_c[:, :], bf_dram.rearrange("(e u) -> e u", u=1))
    for jj in range(4):
        emit_load_perch(prog, sb, cw[:, jj * 8:(jj + 1) * 8], kcw[:, jj])
        emit_load_perch(prog, sb, cw[:, 32 + jj * 8:32 + (jj + 1) * 8], vcw[:, jj])
    for kc in range(NCH):
        prog.dma("y", wfc[:, kc * 16:(kc + 1) * 16], wf_bf[kc * 128:(kc + 1) * 128, :])
    prog.barrier()
    prog.emit("v", lambda e: e.tensor_scalar(dmask[:, :], sb.iota_i[:, :],
                                             sb.iota_colf[:, :], -1e30,
                                             op0=ALU.is_gt, op1=ALU.mult))
    prog.barrier()
    emit_ln(prog, sb, sb.xT, sb.B2)

    def tms(tt):
        return sb.B3[tt // 2][:, (tt % 2) * 1024:(tt % 2 + 1) * 1024]

    def grms_tm(weight_perch=None):
        for tt in range(NTT):
            v3 = tms(tt).rearrange("p (h d) -> p h d", d=64)
            prog.emit("v", lambda e, tt=tt, v3=v3: e.tensor_reduce(
                smC[:, 0:16], v3, axis=AX.X, op=ALU.add) if False else None)
            def st1(e, tt=tt, v3=v3):
                e.tensor_tensor(sb.SCR[:, 0:1024], tms(tt), tms(tt), op=ALU.mult)
                return e.engine_nop()
            prog.emit("v", st1)
            prog.barrier()
            prog.emit("v", lambda e: e.tensor_reduce(
                smC[:, 0:16], sb.SCR[:, 0:1024].rearrange("p (h d) -> p h d", d=64),
                axis=AX.X, op=ALU.add))
            prog.barrier()
            prog.emit("v", lambda e: e.tensor_scalar(smC[:, 0:16], smC[:, 0:16],
                                                     1.0 / 64.0, EPS, op0=ALU.mult,
                                                     op1=ALU.add))
            prog.barrier()
            sid = prog.emit("s", lambda e: e.activation(smC[:, 0:16], smC[:, 0:16],
                                                        AF.Ln), inc=prog.sem("l1_s1"))
            prog.wait("s", prog.sem("l1_s1"), sid)
            prog.barrier()
            sid2 = prog.emit("s", lambda e: e.activation(smC[:, 0:16], smC[:, 0:16],
                                                         AF.Exp, scale=-0.5),
                             inc=prog.sem("l1_s1"))
            prog.wait("v", prog.sem("l1_s1"), sid2)
            prog.barrier()
            def apl(e, tt=tt):
                last = None
                for h in range(16):
                    last = e.tensor_scalar(tms(tt)[:, h * 64:(h + 1) * 64],
                                           tms(tt)[:, h * 64:(h + 1) * 64],
                                           smC[:, h:h + 1], None, op0=ALU.mult)
                return last
            prog.emit("v", apl)
            prog.barrier()

    def conv_silu(src_tiles, wbase):
        for k in range(NCH):
            def cv(e, k=k, wbase=wbase):
                e.tensor_scalar(sb.SCR[:, 0:T], src_tiles[k][:, :],
                                cw[:, wbase + 24 + k:wbase + 25 + k], None, op0=ALU.mult)
                return e.engine_nop()
            prog.emit("v", cv)
            prog.barrier()
            for (j, sh) in ((2, 1), (1, 2), (0, 3)):
                def cv2(e, k=k, j=j, sh=sh, wbase=wbase):
                    return e.scalar_tensor_tensor(
                        sb.SCR[:, sh:T], src_tiles[k][:, 0:T - sh],
                        cw[:, wbase + j * 8 + k:wbase + j * 8 + k + 1],
                        sb.SCR[:, sh:T], op0=ALU.mult, op1=ALU.add)
                prog.emit("v", cv2)
                prog.barrier()
            sid = prog.emit("s", lambda e, k=k: e.activation(
                src_tiles[k][:, :], sb.SCR[:, 0:T], AF.Silu), inc=prog.sem("l1_s2"))
            prog.wait("v", prog.sem("l1_s2"), sid)
            prog.barrier()

    def evq(e, fc, ns, ps):
        return e.tensor_copy(sb.B3[fc][:, ns * 512:(ns + 1) * 512], ps[:, :])
    emit_proj_fm(prog, sb, sb.B2, wq_bf, evq, 1024, 1024)
    def tmb1(tt):
        return sb.B1[tt // 2][:, (tt % 2) * 1024:(tt % 2 + 1) * 1024]
    emit_fm_to_tm(prog, sb, sb.B3, tmb1)
    globals()["_tms_override"] = None
    def grms_on(slotfn):
        nonlocal_data = None
    def grms_tm_on(slotfn):
        for tt in range(NTT):
            def st1(e, tt=tt):
                return e.tensor_tensor(sb.SCR[:, 0:1024], slotfn(tt), slotfn(tt),
                                       op=ALU.mult)
            prog.emit("v", st1)
            prog.barrier()
            prog.emit("v", lambda e: e.tensor_reduce(
                smC[:, 0:16], sb.SCR[:, 0:1024].rearrange("p (h d) -> p h d", d=64),
                axis=AX.X, op=ALU.add))
            prog.barrier()
            prog.emit("v", lambda e: e.tensor_scalar(smC[:, 0:16], smC[:, 0:16],
                                                     1.0 / 64.0, EPS, op0=ALU.mult,
                                                     op1=ALU.add))
            prog.barrier()
            sid = prog.emit("s", lambda e: e.activation(smC[:, 0:16], smC[:, 0:16],
                                                        AF.Ln), inc=prog.sem("l1_s1"))
            prog.wait("s", prog.sem("l1_s1"), sid)
            prog.barrier()
            sid2 = prog.emit("s", lambda e: e.activation(smC[:, 0:16], smC[:, 0:16],
                                                         AF.Exp, scale=-0.5),
                             inc=prog.sem("l1_s1"))
            prog.wait("v", prog.sem("l1_s1"), sid2)
            prog.barrier()
            def apl(e, tt=tt):
                last = None
                for h in range(16):
                    last = e.tensor_scalar(slotfn(tt)[:, h * 64:(h + 1) * 64],
                                           slotfn(tt)[:, h * 64:(h + 1) * 64],
                                           smC[:, h:h + 1], None, op0=ALU.mult)
                return last
            prog.emit("v", apl)
            prog.barrier()
    grms_tm_on(tmb1)
    emit_tm_to_fm(prog, sb, tmb1, sb.B3)
    for k in range(NCH):
        prog.emit("v", lambda e, k=k: e.tensor_scalar(
            sb.B3[k][:, :], sb.B3[k][:, :], qnw_c[:, k:k + 1], 0.125, op0=ALU.mult,
            op1=ALU.mult))
    prog.barrier()
    sQ = prog.sem("l1_qs")
    prog.emit("v", lambda e: e.memset(sb.rep[0:2, 0:T], 1.0))
    prog.barrier()
    for h in range(16):
        kc, ko = h // 2, (h % 2) * 64
        prog.dma("y", q_aug_dram[h][0:64, :], sb.B3[kc][ko:ko + 64, :], sem=sQ)
        prog.dma("y", q_aug_dram[h][64:66, :], sb.rep[0:2, 0:T], sem=sQ)
    prog.barrier()
    sF = prog.sem("l1_f")
    for ns in range(NS):
        def mmf(e, ns=ns):
            last = None
            for kc2 in range(NCH):
                last = e.matmul(sb.mm[ns % 2][0:16, :], wfc[:, kc2 * 16:(kc2 + 1) * 16],
                                sb.B2[kc2][:, ns * 512:(ns + 1) * 512],
                                start=(kc2 == 0), stop=(kc2 == NCH - 1))
            return last
        mid = prog.emit("t", mmf, inc=sF)
        prog.wait("v", sF, mid)
        prog.barrier()
        prog.emit("v", lambda e, ns=ns: e.tensor_scalar(
            sb.SCR[0:16, ns * 512:(ns + 1) * 512], sb.mm[ns % 2][0:16, :],
            bf_c[:, 0:1], None, op0=ALU.add))
        prog.barrier()
    sid = prog.emit("s", lambda e: e.activation(sb.SCR[0:16, 0:T], sb.SCR[0:16, 0:T],
                                                AF.Sigmoid), inc=prog.sem("l1_s3"))
    prog.wait("s", prog.sem("l1_s3"), sid)
    prog.barrier()
    sid2 = prog.emit("s", lambda e: e.activation(sb.SCR[0:16, 0:T], sb.SCR[0:16, 0:T],
                                                 AF.Ln), inc=prog.sem("l1_s3"))
    prog.wait("v", prog.sem("l1_s3"), sid2)
    prog.barrier()
    prog.emit("v", lambda e: e.memset(sb.rep[0:16, 0:T], 1.0))
    prog.barrier()
    prog.emit("v", lambda e: e.tensor_tensor_scan(
        sb.repQ[0:16, 0:T], sb.SCR[0:16, 0:T], sb.rep[0:16, 0:T], 0.0,
        op0=ALU.add, op1=ALU.mult))
    prog.barrier()
    prog.emit("v", lambda e: e.tensor_scalar(sb.SCR[0:16, 0:T], sb.repQ[0:16, 0:T],
                                             -1.0, None, op0=ALU.mult))
    prog.barrier()
    prog.emit("v", lambda e: e.tensor_copy(cumhi[:, :], sb.SCR[0:16, 0:T]))
    prog.barrier()
    prog.emit("v", lambda e: e.tensor_sub(cumlo[:, :], sb.SCR[0:16, 0:T], cumhi[:, :]))
    prog.barrier()
    prog.dma("y", cum_dram[0:16, :], cumhi)
    prog.dma("y", cum_dram[16:32, :], cumlo)
    prog.barrier()
    def evk(e, fc, ns, ps):
        return e.tensor_copy(sb.B3[fc][:, ns * 512:(ns + 1) * 512], ps[:, :])
    emit_proj_fm(prog, sb, sb.B2, wk_bf, evk, 1024, 1024)
    conv_silu(sb.B3, 0)
    emit_fm_to_tm(prog, sb, sb.B3, tmb1)
    grms_tm_on(tmb1)
    emit_tm_to_fm(prog, sb, tmb1, sb.B3)
    for k in range(NCH):
        prog.emit("v", lambda e, k=k: e.tensor_scalar(
            sb.B3[k][:, :], sb.B3[k][:, :], knw_c[:, k:k + 1], None, op0=ALU.mult))
    prog.barrier()
    for h in range(16):
        kc, ko = h // 2, (h % 2) * 64
        prog.dma("y", kaug_dram[h][0:64, :], sb.B3[kc][ko:ko + 64, :], sem=sQ)
        prog.dma("y", kaug_dram[h][64:65, :], cum_dram[h:h + 1, :], sem=sQ)
        prog.dma("y", kaug_dram[h][65:66, :], cum_dram[16 + h:17 + h, :], sem=sQ)
    prog.barrier()
    def evg(e, fc, ns, ps):
        return e.tensor_copy(sb.B3[fc][:, ns * 512:(ns + 1) * 512], ps[:, :])
    emit_proj_fm(prog, sb, sb.B2, wg_bf, evg, 1024, 1024)
    for k in range(NCH):
        prog.dma("y", g_dram[k * 128:(k + 1) * 128, :], sb.B3[k][:, :])
    prog.barrier()
    def evv(e, fc, ns, ps):
        return e.tensor_copy(sb.B3[fc][:, ns * 512:(ns + 1) * 512], ps[:, :])
    emit_proj_fm(prog, sb, sb.B2, wv_bf, evv, 1024, 1024)
    conv_silu(sb.B3, 32)
    def vslot(tt):
        return sb.B2[tt // 2][:, (tt % 2) * 1024:(tt % 2 + 1) * 1024]
    emit_fm_to_tm(prog, sb, sb.B3, vslot)
    sK = prog.sem("l1_kl")
    sQ2 = prog.sem("l1_ql")
    for h in range(16):
        kl = prog.dma("y", kbuf[:, :], kaug_dram[h], sem=sK)
        prog.wait("t", sK, kl)
        prog.barrier()
        for t4 in range(NTT):
            nk = t4 * 128 + 128
            nkc = (nk + 511) // 512
            ql = prog.dma("y", qa[:, :], q_aug_dram[h][:, t4 * 128:(t4 + 1) * 128],
                          sem=sQ2)
            prog.wait("t", sQ2, ql)
            prog.barrier()
            stiles = [sb.mm[0][:, :], sb.mm[1][:, :], sb.flex[:, 0:512],
                      sb.flex[:, 512:1024]]
            for kc2 in range(nkc):
                N = min(512, nk - kc2 * 512)
                def mmS(e, kc2=kc2, N=N):
                    return e.matmul(stiles[kc2][:, 0:N], qa[:, :],
                                    kbuf[:, kc2 * 512:kc2 * 512 + N],
                                    start=True, stop=True)
                mid = prog.emit("t", mmS, inc=prog.sem("l1_sm"))
            prog.wait("v", prog.sem("l1_sm"), prog.sem("l1_sm").n)
            prog.barrier()
            dc = (nk - 128) - (nkc - 1) * 512
            prog.emit("v", lambda e, kc2=nkc - 1, dc=dc: e.tensor_tensor(
                stiles[kc2][:, dc:dc + 128], stiles[kc2][:, dc:dc + 128],
                dmask[:, :], op=ALU.add))
            prog.barrier()
            for kc2 in range(nkc):
                N = min(512, nk - kc2 * 512)
                prog.emit("v", lambda e, kc2=kc2, N=N: e.reduce_max(
                    smC[:, kc2:kc2 + 1], stiles[kc2][:, 0:N], axis=AX.X, negate=True))
            prog.barrier()
            prog.emit("v", lambda e, nkc=nkc: e.tensor_reduce(
                smC[:, 8:9], smC[:, 0:nkc], axis=AX.X, op=ALU.min))
            prog.barrier()
            for kc2 in range(nkc):
                N = min(512, nk - kc2 * 512)
                sid = prog.emit("s", lambda e, kc2=kc2, N=N: e.activation(
                    sb.rep[:, kc2 * 512:kc2 * 512 + N], stiles[kc2][:, 0:N],
                    AF.Exp, bias=smC[:, 8:9]), inc=prog.sem("l1_s4"))
            prog.wait("v", prog.sem("l1_s4"), prog.sem("l1_s4").n)
            prog.barrier()
            prog.emit("v", lambda e, nk=nk: e.reduce_sum(
                smC[:, 9:10], sb.rep[:, 0:nk], axis=AX.X))
            prog.barrier()
            prog.emit("v", lambda e: e.reciprocal(smC[:, 10:11], smC[:, 9:10]))
            prog.barrier()
            prog.emit("v", lambda e, nk=nk: e.tensor_scalar(
                sb.rep[:, 0:nk], sb.rep[:, 0:nk], smC[:, 10:11], None, op0=ALU.mult))
            prog.barrier()
            nkt = nk // 128
            for kt in range(nkt):
                tid = prog.emit("t", lambda e, kt=kt: e.transpose(
                    sb.flexb[:, 0:128], sb.rep[:, kt * 128:(kt + 1) * 128],
                    sb.ident_bf[:, :]), inc=prog.sem("l1_t"))
                prog.wait("v", prog.sem("l1_t"), tid)
                eid = prog.emit("v", lambda e, kt=kt: e.tensor_copy(
                    sb.repQ[:, kt * 128:(kt + 1) * 128], sb.flexb[:, 0:128]),
                    inc=prog.sem("l1_t2"))
                prog.wait("t", prog.sem("l1_t2"), eid)
            prog.barrier()
            def mmAV(e, nkt=nkt, h=h):
                last = None
                for kt in range(nkt):
                    last = e.matmul(sb.big[:, 0:64], sb.repQ[:, kt * 128:(kt + 1) * 128],
                                    vslot(kt)[:, h * 64:(h + 1) * 64],
                                    start=(kt == 0), stop=(kt == nkt - 1))
                return last
            aid = prog.emit("t", mmAV, inc=prog.sem("l1_av"))
            prog.wait("v", prog.sem("l1_av"), aid)
            prog.barrier()
            prog.emit("v", lambda e, t4=t4, h=h: e.tensor_copy(
                tms(t4)[:, h * 64:(h + 1) * 64], sb.big[:, 0:64]))
            prog.barrier()
    grms_tm_on(tms)
    onw_rep = sb.B1[0][:, :].bitcast(F32)
    prog.dma("y", onw_rep, onw_rep_dram)
    prog.barrier()
    for tt in range(NTT):
        prog.emit("v", lambda e, tt=tt: e.tensor_mul(tms(tt), tms(tt), onw_rep))
    prog.barrier()
    for k in range(NCH):
        ld = prog.dma("y", sb.B1[k][:, :], g_dram[k * 128:(k + 1) * 128, :])
    prog.barrier()
    for k in range(NCH):
        sid = prog.emit("s", lambda e, k=k: e.activation(
            sb.B1[k][:, :], sb.B1[k][:, :], AF.Silu), inc=prog.sem("l1_s5"))
    prog.wait("v", prog.sem("l1_s5"), prog.sem("l1_s5").n)
    prog.barrier()
    emit_fm_to_tm(prog, sb, sb.B1, vslot)
    for tt in range(NTT):
        prog.emit("v", lambda e, tt=tt: e.tensor_mul(tms(tt), tms(tt), vslot(tt)))
    prog.barrier()
    emit_tm_to_fm(prog, sb, tms, sb.B1)
    def evo(e, fc, ns, ps):
        sl = slice(ns * 512, (ns + 1) * 512)
        return e.tensor_add(sb.xT[fc][:, sl], sb.xT[fc][:, sl], ps[:, :])
    emit_proj_fm(prog, sb, sb.B1, wo_bf, evo, 1024, 1024)
    prog.barrier()


def build_launch_combine_attn0_pre2(stop_after=None, skip_combine=False,
                                    ctx_out=False, dbg_l0=False):
    nc = bass.Bass("TRN2", target_bir_lowering=False, detect_race_conditions=False)
    d = {}
    def di(name, shape, dt):
        d[name] = nc.dram_tensor(name, shape, dt, kind="ExternalInput").ap()
    di("x", [T, D], F32); di("y_all", [16, 1024, 1040], BF16)
    di("ids_all", [16, 128, 8], F32); di("mybase", [128, 2], F32)
    for n in ("wq", "wk", "wv", "wproj"):
        di(n, [D, D], F32)
    di("w1", [D, 4096], F32); di("w2", [4096, D], F32)
    for n in ("ln1g", "ln1b", "bq", "bk", "bv", "bproj", "ln2g", "ln2b", "b2"):
        di(n, [D], F32)
    di("b1", [4096], F32)
    di("nw", [D], F32); di("wr", [16, D], F32); di("rb", [16], F32)
    x2_o = nc.dram_tensor("x2", [T, D], F32, kind="ExternalOutput").ap()
    xn_o = nc.dram_tensor("xn_tok", [T, D], BF16, kind="ExternalOutput").ap()
    lg_o = nc.dram_tensor("logits", [16, T], F32, kind="ExternalOutput").ap()
    ints = {}
    for n, shape in (("wq_bf", [D, D]), ("wk_bf", [D, D]), ("wv_bf", [D, D]),
                     ("wproj_bf", [D, D]), ("w1_bf", [D, 4096]), ("w2_bf", [4096, D]),
                     ("u_dram", [D, T]), ("qT_dram", [D, T])):
        ints[n] = nc.dram_tensor(n, shape, BF16, kind="Internal").ap()
    ints["ctx_dram"] = nc.dram_tensor(
        "ctxo", [T, D], BF16,
        kind="ExternalOutput" if ctx_out else "Internal").ap()
    acc = nc.dram_tensor("acc", [3072, 1152], BF16, kind="Internal").ap()
    lid16 = nc.dram_tensor("lid16", [128, 128], mybir.dt.int16, kind="Internal").ap()
    prog = Prog(nc)
    sb = alloc_common(prog)
    emit_init_consts(prog, sb)
    emit_cast_weights(prog, sb, [(d[n], ints[n + "_bf"]) for n in
                                 ("wq", "wk", "wv", "wproj", "w1", "w2")])
    emit_load_xT(prog, sb, d["x"])
    mb = sbuf(prog, "mybase_s", [128, 2], F32)
    prog.dma("y", mb[:, :], d["mybase"])
    prog.barrier()
    if not skip_combine:
        emit_moe_combine(prog, sb, [d["y_all"][e] for e in range(16)],
                         [d["ids_all"][e] for e in range(16)], mb[:, 0:1], acc,
                         lid16)
    if stop_after == "combine":
        emit_store_fm(prog, sb, x2_o, sb.xT)
        prog.finalize()
        lower_extended_insts(nc)
        return nc
    dbg = None
    if dbg_l0:
        dbg = (nc.dram_tensor("r2o", [D, T], BF16, kind="ExternalOutput").ap(),
               nc.dram_tensor("wo2", [D, T], BF16, kind="ExternalOutput").ap(),
               nc.dram_tensor("cfo", [D, T], BF16, kind="ExternalOutput").ap(),
               nc.dram_tensor("uo", [D, T], BF16, kind="ExternalOutput").ap())
    emit_l0_attn(prog, sb, ints["wq_bf"], ints["wk_bf"], ints["wv_bf"],
                 ints["wproj_bf"], ints["w1_bf"], ints["w2_bf"],
                 d["ln1g"], d["ln1b"], d["bq"], d["bk"], d["bv"], d["bproj"],
                 d["ln2g"], d["ln2b"], d["b1"], d["b2"],
                 ints["u_dram"], ints["qT_dram"], ints["ctx_dram"], dbg=dbg)
    emit_store_fm(prog, sb, x2_o, sb.xT)
    emit_pre_moe(prog, sb, sb.xT, d["nw"], d["wr"], d["rb"], xn_o, lg_o)
    prog.finalize()
    lower_extended_insts(nc)
    return nc


def build_launch_combine_l1(stop_after=None, skip_combine=False, dbg_w=False):
    nc = bass.Bass("TRN2", target_bir_lowering=False, detect_race_conditions=False)
    d = {}
    def di(name, shape, dt):
        d[name] = nc.dram_tensor(name, shape, dt, kind="ExternalInput").ap()
    di("x", [T, D], F32); di("y_all", [16, 1024, 1040], BF16)
    di("ids_all", [16, 128, 8], F32); di("mybase", [128, 2], F32)
    di("k0", [2048, 4096], U8); di("k1", [4096, 4096], U8); di("k2", [4096, 1024], U8)
    di("scol", [128, 6], F32)
    for n in ("wq", "wk", "wv", "wo", "wg"):
        di(n, [D, D], F32)
    di("wf", [D, 16], F32); di("bf", [16], F32)
    di("qnw", [D], F32); di("knw", [D], F32); di("onw_rep", [128, D], F32)
    di("kcw", [D, 4], F32); di("vcw", [D, 4], F32)
    out_o = nc.dram_tensor("out", [T, D], F32, kind="ExternalOutput").ap()
    ints = {}
    for n, shape in (("w0p", [D, 4096]), ("w1p", [4096, 4096]), ("w2p", [4096, D]),
                     ("wq_bf", [D, D]), ("wk_bf", [D, D]), ("wv_bf", [D, D]),
                     ("wo_bf", [D, D]), ("wg_bf", [D, D]), ("wf_bf", [D, 16]),
                     ("g_dram", [D, T])):
        kind = "ExternalOutput" if (dbg_w and n in ("w0p", "w1p", "w2p")) else "Internal"
        ints[n] = nc.dram_tensor(n, shape, BF16, kind=kind).ap()
    q_aug = nc.dram_tensor("q_aug", [16, 66, T], BF16, kind="Internal").ap()
    kaug = nc.dram_tensor("kaug", [16, 66, T], BF16, kind="Internal").ap()
    acc = nc.dram_tensor("acc", [3072, 1152], BF16, kind="Internal").ap()
    lid16 = nc.dram_tensor("lid16", [128, 128], mybir.dt.int16, kind="Internal").ap()
    prog = Prog(nc)
    sb = alloc_common(prog)
    emit_init_consts(prog, sb)
    emit_cast_weights(prog, sb, [(d[n], ints[n + "_bf"]) for n in
                                 ("wq", "wk", "wv", "wo", "wg", "wf")])
    emit_load_xT(prog, sb, d["x"])
    mb = sbuf(prog, "mybase_s", [128, 2], F32)
    scol = sbuf(prog, "scol_s", [128, 6], F32)
    prog.dma("y", mb[:, :], d["mybase"])
    prog.dma("y", scol[:, :], d["scol"])
    prog.barrier()
    if not skip_combine:
        emit_moe_combine(prog, sb, [d["y_all"][e] for e in range(16)],
                         [d["ids_all"][e] for e in range(16)], mb[:, 0:1], acc,
                         lid16)
    if stop_after == "combine":
        emit_store_fm(prog, sb, out_o, sb.xT)
        prog.finalize()
        lower_extended_insts(nc)
        return nc
    emit_l1ffn_prep(prog, sb, d["k0"], d["k1"], d["k2"],
                    ints["w0p"], ints["w1p"], ints["w2p"])
    dbg_h = None
    if dbg_w:
        dbg_h = nc.dram_tensor("h1o", [4096, 512], BF16, kind="ExternalOutput").ap()
    emit_l1_ffn(prog, sb, ints["w0p"], ints["w1p"], ints["w2p"], scol, dbg_h=dbg_h)
    if stop_after == "ffn1":
        emit_store_fm(prog, sb, out_o, sb.xT)
        prog.finalize()
        lower_extended_insts(nc)
        return nc
    emit_l1_attn(prog, sb, ints["wq_bf"], ints["wk_bf"], ints["wv_bf"],
                 ints["wo_bf"], ints["wg_bf"], ints["wf_bf"], d["bf"],
                 d["qnw"], d["knw"], d["onw_rep"], d["kcw"], d["vcw"],
                 q_aug, kaug, ints["g_dram"])
    if stop_after == "attn":
        emit_store_fm(prog, sb, out_o, sb.xT)
        prog.finalize()
        lower_extended_insts(nc)
        return nc
    emit_l1_ffn(prog, sb, ints["w0p"], ints["w1p"], ints["w2p"], scol)
    emit_store_fm(prog, sb, out_o, sb.xT)
    prog.finalize()
    lower_extended_insts(nc)
    return nc


from concourse.bass_utils import run_bass_kernel_spmd

_CACHE = {}


def _programs():
    if "p" not in _CACHE:
        _CACHE["p"] = (build_launch_pre1(), build_launch_experts(),
                       build_launch_combine_attn0_pre2(), build_launch_combine_l1())
    return _CACHE["p"]


def kernel(**inp):
    f32 = np.float32
    def g(k):
        return np.ascontiguousarray(np.asarray(inp[k], f32))
    X = g("L0_attn_hidden_states")
    nw = g("L0_ffn_norm_weight"); wr = g("L0_ffn_router_weight")
    rb = g("L0_ffn_router_bias")
    fc1 = g("L0_ffn_fc1_weight"); fc1b = g("L0_ffn_fc1_bias")
    fc2 = g("L0_ffn_fc2_weight"); fc2b = g("L0_ffn_fc2_bias")
    nc1, nc2, nc3, nc5 = _programs()
    CORES = list(range(8))

    def run(nc, per_core):
        return run_bass_kernel_spmd(nc, per_core, core_ids=CORES).results

    def moe_round(xns, lgs):
        xn_full = np.ascontiguousarray(np.concatenate(xns, axis=0))
        lg_tok = np.ascontiguousarray(
            np.concatenate([l.T for l in lgs], axis=0)).astype(f32)
        ins = []
        for c in CORES:
            lg_my = np.stack([np.concatenate([lgs[cc][2 * c + j] for cc in CORES])
                              for j in (0, 1)]).astype(f32)
            esel = np.zeros((2, 128, 16), f32)
            esel[0, :, 2 * c] = 1.0
            esel[1, :, 2 * c + 1] = 1.0
            ins.append(dict(xn=xn_full, lg_my=lg_my, lg_tok=lg_tok, esel=esel,
                            fc1=fc1[2 * c:2 * c + 2], fc1b=fc1b[2 * c:2 * c + 2],
                            fc2=fc2[2 * c:2 * c + 2], fc2b=fc2b[2 * c:2 * c + 2]))
        r = run(nc2, ins)
        y_all = np.stack([np.asarray(r[e // 2]["y"])[e % 2] for e in range(16)])
        ids_all = np.stack([np.asarray(r[e // 2]["ids"], f32)[e % 2]
                            for e in range(16)])
        return np.ascontiguousarray(y_all), np.ascontiguousarray(ids_all)

    r1 = run(nc1, [dict(x=X[c], nw=nw, wr=wr, rb=rb) for c in CORES])
    xns = [np.asarray(r1[c]["xn_tok"]) for c in CORES]
    lgs = [np.asarray(r1[c]["logits"], f32) for c in CORES]
    y_all, ids_all = moe_round(xns, lgs)
    l0w = dict(wq=g("L0_attn_wq"), wk=g("L0_attn_wk"), wv=g("L0_attn_wv"),
               wproj=g("L0_attn_wproj"), w1=g("L0_attn_w1"), w2=g("L0_attn_w2"),
               ln1g=g("L0_attn_ln1_g"), ln1b=g("L0_attn_ln1_b"),
               bq=g("L0_attn_bq"), bk=g("L0_attn_bk"), bv=g("L0_attn_bv"),
               bproj=g("L0_attn_bproj"), ln2g=g("L0_attn_ln2_g"),
               ln2b=g("L0_attn_ln2_b"), b1=g("L0_attn_b1"), b2=g("L0_attn_b2"),
               nw=nw, wr=wr, rb=rb)
    ins3 = [dict(x=X[c], y_all=y_all, ids_all=ids_all,
                 mybase=np.full((128, 2), 2048.0 * c, f32), **l0w) for c in CORES]
    r3 = run(nc3, ins3)
    x2 = [np.asarray(r3[c]["x2"], f32) for c in CORES]
    xns2 = [np.asarray(r3[c]["xn_tok"]) for c in CORES]
    lgs2 = [np.asarray(r3[c]["logits"], f32) for c in CORES]
    y_all2, ids_all2 = moe_round(xns2, lgs2)
    k0u = np.ascontiguousarray(np.asarray(inp["L1_ffn_k0"], bool)).view(np.uint8)
    k1u = np.ascontiguousarray(np.asarray(inp["L1_ffn_k1"], bool)).view(np.uint8)
    k2u = np.ascontiguousarray(np.asarray(inp["L1_ffn_k2"], bool)).view(np.uint8)
    scolm = np.empty((128, 6), f32)
    scolm[:, 0:2] = float(np.asarray(inp["L1_ffn_s0"]))
    scolm[:, 2:4] = float(np.asarray(inp["L1_ffn_s1"]))
    scolm[:, 4:6] = float(np.asarray(inp["L1_ffn_s2"]))
    onw_rep = np.ascontiguousarray(
        np.broadcast_to(np.tile(g("L1_attn_onw"), 16), (128, 1024)), f32)
    l1w = dict(k0=k0u, k1=k1u, k2=k2u, scol=scolm,
               wq=g("L1_attn_Wq"), wk=g("L1_attn_Wk"), wv=g("L1_attn_Wv"),
               wo=g("L1_attn_Wo"), wg=g("L1_attn_Wg"), wf=g("L1_attn_Wf"),
               bf=g("L1_attn_bf"), qnw=g("L1_attn_qnw"), knw=g("L1_attn_knw"),
               onw_rep=onw_rep, kcw=g("L1_attn_kconv_w"), vcw=g("L1_attn_vconv_w"))
    ins5 = [dict(x=x2[c], y_all=y_all2, ids_all=ids_all2,
                 mybase=np.full((128, 2), 2048.0 * c, f32), **l1w) for c in CORES]
    r5 = run(nc5, ins5)
    return np.stack([np.asarray(r5[c]["out"], f32) for c in CORES])
